# version 02
# baseline (speedup 1.0000x reference)
import numpy as np
from contextlib import ExitStack
import concourse.bass as bass
import concourse.mybir as mybir
from concourse.bass_utils import run_bass_kernel_spmd

F32 = mybir.dt.float32
BF16 = mybir.dt.bfloat16
AF = mybir.ActivationFunctionType
ALU = mybir.AluOpType
AX = mybir.AxisListType

NCORE = 8
B, S, D = 4, 4096, 1024
DIN, DST = 2048, 16
TPC = S // 2
CH = 512
NCH = TPC // CH
KD = D // 128
KI = DIN // 128


class Prog:
    SEMNAMES = ("tk_sync", "tk_vector", "tk_scalar", "tk_tensor", "tk_gpsimd",
                "dms", "dmg", "dmw", "dwa", "dwb", "dxa", "dxb", "mm", "ev")
    ENGINES = ("sync", "vector", "scalar", "tensor", "gpsimd")

    def __init__(self, nc, ctx):
        self.nc = nc
        self.sems = {n: ctx.enter_context(nc.semaphore(n)) for n in self.SEMNAMES}
        self.tot = {n: 0 for n in self.SEMNAMES}
        self.segs = []

    def seg(self, engine, provides=None, waits=None):
        def deco(fn):
            self.segs.append((engine, fn, dict(waits if waits is not None
                                               else self.tot)))
            for k, v in (provides or {}).items():
                if k == "tick":
                    k = "tk_" + engine
                self.tot[k] += v
            return fn
        return deco

    def base(self, sem):
        return self.tot[sem]

    def build(self, ctx):
        self.segs.append(("sync", lambda eng, P: None, dict(self.tot)))
        block = ctx.enter_context(self.nc.Block())
        for en in self.ENGINES:
            fns = [(fn, waits) for (e, fn, waits) in self.segs if e == en]
            if not fns:
                continue

            def body(eng, fns=fns, en=en):
                self.sems["tick"] = self.sems["tk_" + en]
                for fn, waits in fns:
                    for sname, v in waits.items():
                        if v > 0:
                            eng.wait_ge(self.sems[sname], v)
                    fn(eng, self)
            getattr(block, en)(body)


def _dma(eng, P, sem, out, in_):
    eng.dma_start(out=out, in_=in_).then_inc(P.sems[sem], 16)


def emit_stream_mm(p, src, bufs, Mt, Kt, N, rhs, evac, ps, group_mt,
                   nb=4, tile_cols=128):
    ngroups = (Mt + group_mt - 1) // group_mt
    mm0, ev0 = p.base("mm"), p.base("ev")
    dw0 = (p.base("dwa"), p.base("dwb"))
    waits0 = dict(p.tot)

    def gload(eng, P):
        for g in range(ngroups):
            if g >= 2:
                eng.wait_ge(P.sems["mm"], mm0 + (g - 1) * group_mt)
            _dma(eng, P, ("dwa", "dwb")[g % 2], bufs[g % 2][:], src(g))
    na = 16 * ((ngroups + 1) // 2)
    nb_ = 16 * (ngroups // 2)
    p.seg("gpsimd", provides={"dwa": na, "dwb": nb_}, waits=waits0)(gload)

    def tseg(eng, P):
        for g in range(ngroups):
            eng.wait_ge(P.sems[("dwa", "dwb")[g % 2]],
                        dw0[g % 2] + 16 * (g // 2 + 1))
            for mi_l in range(min(group_mt, Mt - g * group_mt)):
                mi = g * group_mt + mi_l
                if mi >= nb:
                    eng.wait_ge(P.sems["ev"], ev0 + mi - nb + 1)
                bank = ps[mi % nb]
                for ki in range(Kt):
                    ins = eng.matmul(
                        bank[:, 0:N],
                        bufs[g % 2][:, (mi_l * Kt + ki) * tile_cols:
                                    (mi_l * Kt + ki + 1) * tile_cols],
                        rhs(ki), start=(ki == 0), stop=(ki == Kt - 1))
                ins.then_inc(P.sems["mm"], 1)
    p.seg("tensor", provides={"mm": Mt}, waits=waits0)(tseg)

    def sseg(eng, P):
        for mi in range(Mt):
            eng.wait_ge(P.sems["mm"], mm0 + mi + 1)
            evac(mi, eng, ps[mi % nb]).then_inc(P.sems["ev"], 1)
    p.seg("scalar", provides={"ev": Mt}, waits=waits0)(sseg)


def emit_res_mm(p, wsb, Mt, Kt, N, rhs, evac, ps, nb=4, mm_extra=None):
    mm0, ev0 = p.base("mm"), p.base("ev")
    waits0 = dict(p.tot)

    def tseg(eng, P):
        for mi in range(Mt):
            if mi >= nb:
                eng.wait_ge(P.sems["ev"], ev0 + mi - nb + 1)
            bank = ps[mi % nb]
            for ki in range(Kt):
                ins = eng.matmul(bank[:, 0:N], wsb(ki, mi), rhs(ki),
                                 start=(ki == 0),
                                 stop=(ki == Kt - 1 and mm_extra is None))
            if mm_extra is not None:
                ins = mm_extra(mi, eng, bank)
            ins.then_inc(P.sems["mm"], 1)
    p.seg("tensor", provides={"mm": Mt}, waits=waits0)(tseg)

    def sseg(eng, P):
        for mi in range(Mt):
            eng.wait_ge(P.sems["mm"], mm0 + mi + 1)
            evac(mi, eng, ps[mi % nb]).then_inc(P.sems["ev"], 1)
    p.seg("scalar", provides={"ev": Mt}, waits=waits0)(sseg)


def emit_ln(p, nk, N, xbf, sq_buf, rows, ps_s1, ps_s2, ps_rR, ps_rM, write,
            tmp, ones_col, ones_row, mode="plain", eps=1e-5, eps_ap=None,
            sq_ping=False):
    DD = nk * 128
    r = lambda nm: rows[nm][0:1, 0:N]

    if not sq_ping:
        def squares(eng, P):
            for ki in range(nk):
                ins = eng.activation(out=sq_buf(ki), in_=xbf(ki), func=AF.Square)
            ins.then_inc(P.sems["tick"], 1)
        p.seg("scalar", provides={"tick": 1})(squares)

        def stats(eng, P):
            for ki in range(nk):
                eng.matmul(ps_s1[0:1, 0:N], ones_col[:, 0:1], xbf(ki),
                           start=(ki == 0), stop=(ki == nk - 1))
            for ki in range(nk):
                ins = eng.matmul(ps_s2[0:1, 0:N], ones_col[:, 0:1], sq_buf(ki),
                                 start=(ki == 0), stop=(ki == nk - 1))
            ins.then_inc(P.sems["mm"], 1)
        p.seg("tensor", provides={"mm": 1})(stats)
    else:
        mm0, ev0 = p.base("mm"), p.base("ev")
        w0 = dict(p.tot)

        def squares(eng, P):
            for ki in range(nk):
                if ki >= 2:
                    eng.wait_ge(P.sems["mm"], mm0 + ki - 1)
                eng.activation(out=sq_buf(ki), in_=xbf(ki),
                               func=AF.Square).then_inc(P.sems["ev"], 1)
        p.seg("scalar", provides={"ev": nk}, waits=w0)(squares)

        def stats(eng, P):
            for ki in range(nk):
                eng.matmul(ps_s1[0:1, 0:N], ones_col[:, 0:1], xbf(ki),
                           start=(ki == 0), stop=(ki == nk - 1))
            for ki in range(nk):
                eng.wait_ge(P.sems["ev"], ev0 + ki + 1)
                eng.matmul(ps_s2[0:1, 0:N], ones_col[:, 0:1], sq_buf(ki),
                           start=(ki == 0),
                           stop=(ki == nk - 1)).then_inc(P.sems["mm"], 1)
        p.seg("tensor", provides={"mm": nk}, waits=w0)(stats)

    def rows1(eng, P):
        eng.tensor_scalar_mul(out=r("mu"), in0=ps_s1[0:1, 0:N], scalar1=1.0 / DD)
        eng.drain()
        if mode == "l0final":
            eng.tensor_mul(out=r("sd"), in0=r("mu"), in1=ps_s1[0:1, 0:N])
            eng.drain()
            eng.tensor_sub(out=r("ex2"), in0=ps_s2[0:1, 0:N], in1=r("sd"))
            eng.drain()
            ins = eng.tensor_scalar_mul(out=r("var"), in0=r("ex2"),
                                        scalar1=1.0 / (DD - 1))
        else:
            eng.tensor_scalar_mul(out=r("ex2"), in0=ps_s2[0:1, 0:N],
                                  scalar1=1.0 / DD)
            eng.tensor_mul(out=r("sd"), in0=r("mu"), in1=r("mu"))
            eng.drain()
            ins = eng.tensor_sub(out=r("var"), in0=r("ex2"), in1=r("sd"))
        ins.then_inc(P.sems["tick"], 1)
    p.seg("vector", provides={"tick": 1})(rows1)

    def rows2(eng, P):
        if mode == "l0final":
            ins = eng.activation(out=r("sd"), in_=r("var"), func=AF.Sqrt)
        else:
            ins = eng.activation(out=r("sd"), in_=r("var"), func=AF.Sqrt,
                                 bias=eps_ap)
        ins.then_inc(P.sems["tick"], 1)
    p.seg("scalar", provides={"tick": 1})(rows2)

    def rows3(eng, P):
        if mode == "l0final":
            eng.tensor_scalar_add(out=r("ex2"), in0=r("sd"), scalar1=float(eps))
            eng.drain()
            eng.reciprocal(out=r("r"), in_=r("ex2"))
        else:
            eng.reciprocal(out=r("r"), in_=r("sd"))
        eng.drain()
        ins = eng.tensor_mul(out=r("mur"), in0=r("mu"), in1=r("r"))
        ins.then_inc(P.sems["tick"], 1)
    p.seg("vector", provides={"tick": 1})(rows3)

    def reps(eng, P):
        eng.matmul(ps_rR[:, 0:N], ones_row[0:1, :], r("r"), start=True, stop=True)
        ins = eng.matmul(ps_rM[:, 0:N], ones_row[0:1, :], r("mur"),
                         start=True, stop=True)
        ins.then_inc(P.sems["mm"], 1)
    p.seg("tensor", provides={"mm": 1})(reps)

    def norm(eng, P):
        for ki in range(nk):
            t = tmp(ki)
            eng.tensor_mul(out=t, in0=xbf(ki), in1=ps_rR[:, 0:N])
            eng.drain()
            ins = write(ki, eng, t, ps_rM[:, 0:N])
            eng.drain()
        ins.then_inc(P.sems["tick"], 1)
    p.seg("vector", provides={"tick": 1})(norm)


def build_l1():
    nc = bass.Bass("TRN2", target_bir_lowering=False)
    x_part = nc.dram_tensor("x_part", [D, TPC], F32, kind="ExternalInput").ap()
    x_halo = nc.dram_tensor("x_halo", [D, 4], F32, kind="ExternalInput").ap()
    halo_mask = nc.dram_tensor("halo_mask", [128, 1], F32, kind="ExternalInput").ap()
    Win_t = nc.dram_tensor("Win_t", [128, 32 * KD * 128], F32, kind="ExternalInput").ap()
    Wdt_t = nc.dram_tensor("Wdt_t", [128, KI * KI * 128], F32, kind="ExternalInput").ap()
    Wxp_t = nc.dram_tensor("Wxp_t", [128, KI * 32], F32, kind="ExternalInput").ap()
    b_in_x = nc.dram_tensor("b_in_x", [128, KI], F32, kind="ExternalInput").ap()
    b_in_z = nc.dram_tensor("b_in_z", [128, KI], F32, kind="ExternalInput").ap()
    conv_w_p = nc.dram_tensor("conv_w_p", [128, KI * 4], F32, kind="ExternalInput").ap()
    conv_b_p = nc.dram_tensor("conv_b_p", [128, KI], F32, kind="ExternalInput").ap()
    b_dt_p = nc.dram_tensor("b_dt_p", [128, KI], F32, kind="ExternalInput").ap()
    b_xp_p = nc.dram_tensor("b_xp_p", [128, 1], F32, kind="ExternalInput").ap()

    delta_o = nc.dram_tensor("delta_o", [DIN, TPC], F32, kind="ExternalOutput").ap()
    xconv_o = nc.dram_tensor("xconv_o", [DIN, TPC], BF16, kind="ExternalOutput").ap()
    siluz_o = nc.dram_tensor("siluz_o", [DIN, TPC], BF16, kind="ExternalOutput").ap()
    bc_o = nc.dram_tensor("bc_o", [32, TPC], F32, kind="ExternalOutput").ap()

    with ExitStack() as ctx:
        E = ctx.enter_context
        p = Prog(nc, ctx)
        sb = lambda nm, sh, dt=F32: E(nc.sbuf_tensor(nm, sh, dt))

        win_b = [sb(f"win{i}", [128, 4 * KD * 128], BF16) for i in range(2)]
        wdt_b = [sb(f"wdt{i}", [128, 2 * KI * 128], BF16) for i in range(2)]
        wxp = sb("wxp", [128, KI * 32], BF16)
        xin = sb("xin", [128, KI * (CH + 3)], BF16)
        x_bf = sb("x_bf", [128, KD * CH], BF16)
        xsq = sb("xsq", [128, KD * CH], BF16)
        xn_bf = sb("xn_bf", [128, KD * CH], BF16)
        A_sz = sb("A_sz", [128, KI * CH], F32)
        B_acc = sb("B_acc", [128, KI * CH], F32)
        B_bf = sb("B_bf", [128, KI * CH], BF16)
        C_dp = sb("C_dp", [128, KI * CH], F32)
        halo_st = sb("halo_st", [128, KI * 4], F32)
        xh_bf = sb("xh_bf", [128, KD * 4], BF16)
        xhsq = sb("xhsq", [128, KD * 4], BF16)
        xnh_bf = sb("xnh_bf", [128, KD * 4], BF16)
        hm_sb = sb("hm_sb", [128, 1], F32)
        vecs = {nm: sb(nm + "_s", [128, KI]) for nm in
                ("binx", "binz", "convb", "bdt")}
        convw = sb("convw_s", [128, KI * 4], F32)
        bxp = sb("bxp_s", [128, 1], F32)
        ones_col = sb("ones_col", [128, 1], BF16)
        ones_row = sb("ones_row", [1, 128], F32)
        cst = sb("cst", [1, 4], F32)
        rows = {nm: sb("row_" + nm, [1, CH]) for nm in
                ("mu", "ex2", "var", "sd", "r", "mur")}
        hrows = {nm: sb("hrow_" + nm, [1, 4]) for nm in
                 ("mu", "ex2", "var", "sd", "r", "mur")}
        tmp_t = sb("tmp_t", [128, CH], F32)
        tmp_h = sb("tmp_h", [128, 4], F32)

        ps = [E(nc.psum_tensor(f"ps{i}", [128, CH], F32)) for i in range(4)]
        ps_s1 = E(nc.psum_tensor("ps_s1", [128, CH], F32))
        ps_s2 = E(nc.psum_tensor("ps_s2", [128, CH], F32))
        ps_rR = E(nc.psum_tensor("ps_rR", [128, CH], F32))
        ps_rM = E(nc.psum_tensor("ps_rM", [128, CH], F32))

        def init_loads(eng, P):
            for nm, src in (("binx", b_in_x), ("binz", b_in_z),
                            ("convb", conv_b_p), ("bdt", b_dt_p)):
                _dma(eng, P, "dms", vecs[nm][:], src)
            _dma(eng, P, "dms", convw[:], conv_w_p)
            _dma(eng, P, "dms", bxp[:], b_xp_p)
            _dma(eng, P, "dms", hm_sb[:], halo_mask)
        p.seg("sync", provides={"dms": 16 * 7})(init_loads)

        def init_wxp(eng, P):
            _dma(eng, P, "dmw", wxp[:], Wxp_t)
            for ki in range(KD):
                _dma(eng, P, "dmg", xh_bf[:, ki * 4:(ki + 1) * 4],
                     x_halo[ki * 128:(ki + 1) * 128, :])
        p.seg("gpsimd", provides={"dmw": 16, "dmg": 16 * KD})(init_wxp)

        def init_ones(eng, P):
            eng.memset(ones_col[:], 1.0)
            eng.memset(ones_row[:], 1.0)
            eng.memset(cst[0:1, 0:1], 1e-5).then_inc(P.sems["tick"], 1)
        p.seg("vector", provides={"tick": 1})(init_ones)

        emit_ln(p, KD, 4,
                xbf=lambda ki: xh_bf[:, ki * 4:(ki + 1) * 4],
                sq_buf=lambda ki: xhsq[:, ki * 4:(ki + 1) * 4],
                rows=hrows, ps_s1=ps_s1, ps_s2=ps_s2, ps_rR=ps_rR, ps_rM=ps_rM,
                write=lambda ki, eng, t, psM: eng.tensor_sub(
                    out=xnh_bf[:, ki * 4:(ki + 1) * 4], in0=t, in1=psM),
                tmp=lambda ki: tmp_h[:, 0:4],
                ones_col=ones_col, ones_row=ones_row, mode="plain",
                eps_ap=cst[0:1, 0:1])

        emit_stream_mm(
            p, src=lambda g: Win_t[:, (g * 4) * KD * 128:(g + 1) * 4 * KD * 128],
            bufs=win_b, Mt=KI, Kt=KD, N=4,
            rhs=lambda ki: xnh_bf[:, ki * 4:(ki + 1) * 4],
            evac=lambda mi, eng, bank: eng.activation(
                out=halo_st[:, mi * 4:(mi + 1) * 4], in_=bank[:, 0:4],
                func=AF.Identity, bias=vecs["binx"][:, mi:mi + 1]),
            ps=ps, group_mt=4)

        def halo_fix(eng, P):
            for ti in range(KI):
                ins = eng.tensor_scalar_mul(
                    out=xin[:, ti * (CH + 3):ti * (CH + 3) + 3],
                    in0=halo_st[:, ti * 4:ti * 4 + 3],
                    scalar1=hm_sb[:, 0:1])
            ins.then_inc(P.sems["tick"], 1)
        p.seg("vector", provides={"tick": 1})(halo_fix)

        for ch in range(NCH):
            def x_load(eng, P, ch=ch):
                for ki in range(KD):
                    _dma(eng, P, "dmg", x_bf[:, ki * CH:(ki + 1) * CH],
                         x_part[ki * 128:(ki + 1) * 128, ch * CH:(ch + 1) * CH])
            p.seg("gpsimd", provides={"dmg": 16 * KD})(x_load)

            emit_ln(p, KD, CH,
                    xbf=lambda ki: x_bf[:, ki * CH:(ki + 1) * CH],
                    sq_buf=lambda ki: xsq[:, ki * CH:(ki + 1) * CH],
                    rows=rows, ps_s1=ps_s1, ps_s2=ps_s2, ps_rR=ps_rR, ps_rM=ps_rM,
                    write=lambda ki, eng, t, psM: eng.tensor_sub(
                        out=xn_bf[:, ki * CH:(ki + 1) * CH], in0=t, in1=psM),
                    tmp=lambda ki: A_sz[:, 0:CH],
                    ones_col=ones_col, ones_row=ones_row, mode="plain",
                    eps_ap=cst[0:1, 0:1])

            def win_evac(mi, eng, bank):
                if mi < KI:
                    return eng.activation(
                        out=xin[:, mi * (CH + 3) + 3: mi * (CH + 3) + 3 + CH],
                        in_=bank[:, 0:CH], func=AF.Identity,
                        bias=vecs["binx"][:, mi:mi + 1])
                ti = mi - KI
                return eng.activation(
                    out=A_sz[:, ti * CH:(ti + 1) * CH],
                    in_=bank[:, 0:CH], func=AF.Silu,
                    bias=vecs["binz"][:, ti:ti + 1])

            emit_stream_mm(
                p, src=lambda g: Win_t[:, (g * 4) * KD * 128:(g + 1) * 4 * KD * 128],
                bufs=win_b, Mt=32, Kt=KD, N=CH,
                rhs=lambda ki: xn_bf[:, ki * CH:(ki + 1) * CH],
                evac=win_evac, ps=ps, group_mt=4)

            def sz_store(eng, P, ch=ch):
                for ti in range(KI):
                    _dma(eng, P, "dmg",
                         siluz_o[ti * 128:(ti + 1) * 128, ch * CH:(ch + 1) * CH],
                         A_sz[:, ti * CH:(ti + 1) * CH])
            p.seg("gpsimd", provides={"dmg": 16 * KI})(sz_store)

            def conv_v(eng, P):
                for ti in range(KI):
                    base = ti * (CH + 3)
                    bsl = slice(ti * CH, (ti + 1) * CH)
                    eng.tensor_scalar_mul(
                        out=B_acc[:, bsl], in0=xin[:, base:base + CH],
                        scalar1=convw[:, ti * 4:ti * 4 + 1])
                    eng.drain()
                    eng.scalar_tensor_tensor(
                        out=C_dp[:, bsl], in0=xin[:, base + 1:base + 1 + CH],
                        scalar=convw[:, ti * 4 + 1:ti * 4 + 2],
                        in1=B_acc[:, bsl], op0=ALU.mult, op1=ALU.add)
                    eng.drain()
                    eng.scalar_tensor_tensor(
                        out=B_acc[:, bsl], in0=xin[:, base + 2:base + 2 + CH],
                        scalar=convw[:, ti * 4 + 2:ti * 4 + 3],
                        in1=C_dp[:, bsl], op0=ALU.mult, op1=ALU.add)
                    eng.drain()
                    ins = eng.scalar_tensor_tensor(
                        out=C_dp[:, bsl], in0=xin[:, base + 3:base + 3 + CH],
                        scalar=convw[:, ti * 4 + 3:ti * 4 + 4],
                        in1=B_acc[:, bsl], op0=ALU.mult, op1=ALU.add)
                    eng.drain()
                ins.then_inc(P.sems["tick"], 1)
            p.seg("vector", provides={"tick": 1})(conv_v)

            def conv_s(eng, P):
                for ti in range(KI):
                    ins = eng.activation(
                        out=B_bf[:, ti * CH:(ti + 1) * CH],
                        in_=C_dp[:, ti * CH:(ti + 1) * CH], func=AF.Silu,
                        bias=vecs["convb"][:, ti:ti + 1])
                ins.then_inc(P.sems["tick"], 1)
            p.seg("scalar", provides={"tick": 1})(conv_s)

            if ch < NCH - 1:
                def carry(eng, P):
                    for ti in range(KI):
                        base = ti * (CH + 3)
                        ins = eng.tensor_copy(
                            out=xin[:, base:base + 3],
                            in_=xin[:, base + CH:base + CH + 3])
                    ins.then_inc(P.sems["tick"], 1)
                p.seg("vector", provides={"tick": 1})(carry)

            def xc_store(eng, P, ch=ch):
                for ti in range(KI):
                    _dma(eng, P, "dms",
                         xconv_o[ti * 128:(ti + 1) * 128, ch * CH:(ch + 1) * CH],
                         B_bf[:, ti * CH:(ti + 1) * CH])
            p.seg("sync", provides={"dms": 16 * KI})(xc_store)

            def xp_t(eng, P):
                for ki in range(KI):
                    ins = eng.matmul(ps_s2[0:32, 0:CH],
                                     wxp[:, ki * 32:(ki + 1) * 32],
                                     B_bf[:, ki * CH:(ki + 1) * CH],
                                     start=(ki == 0), stop=(ki == KI - 1))
                ins.then_inc(P.sems["mm"], 1)
            p.seg("tensor", provides={"mm": 1})(xp_t)

            def xp_ev(eng, P):
                eng.activation(out=tmp_t[0:32, 0:CH], in_=ps_s2[0:32, 0:CH],
                               func=AF.Identity,
                               bias=bxp[0:32, 0:1]).then_inc(P.sems["ev"], 1)
            p.seg("scalar", provides={"ev": 1})(xp_ev)

            def bc_store(eng, P, ch=ch):
                _dma(eng, P, "dms", bc_o[:, ch * CH:(ch + 1) * CH],
                     tmp_t[0:32, 0:CH])
            p.seg("sync", provides={"dms": 16})(bc_store)

            emit_stream_mm(
                p, src=lambda g: Wdt_t[:, (g * 2) * KI * 128:(g + 1) * 2 * KI * 128],
                bufs=wdt_b, Mt=KI, Kt=KI, N=CH,
                rhs=lambda ki: B_bf[:, ki * CH:(ki + 1) * CH],
                evac=lambda mi, eng, bank: eng.activation(
                    out=C_dp[:, mi * CH:(mi + 1) * CH], in_=bank[:, 0:CH],
                    func=AF.Identity, bias=vecs["bdt"][:, mi:mi + 1]),
                ps=ps, group_mt=2)

            def sp_s(eng, P):
                for ti in range(KI):
                    sl = slice(ti * CH, (ti + 1) * CH)
                    eng.activation(out=A_sz[:, sl], in_=C_dp[:, sl], func=AF.Abs)
                    eng.drain()
                    eng.activation(out=A_sz[:, sl], in_=A_sz[:, sl], func=AF.Exp,
                                   scale=-1.0)
                    eng.drain()
                    ins = eng.activation(out=A_sz[:, sl], in_=A_sz[:, sl],
                                         func=AF.Ln, bias=1.0)
                    eng.drain()
                ins.then_inc(P.sems["tick"], 1)
            p.seg("scalar", provides={"tick": 1})(sp_s)

            def sp_v(eng, P):
                for ti in range(KI):
                    sl = slice(ti * CH, (ti + 1) * CH)
                    eng.tensor_scalar_max(out=B_acc[:, sl], in0=C_dp[:, sl],
                                          scalar1=0.0)
                    eng.drain()
                    ins = eng.tensor_add(out=C_dp[:, sl], in0=A_sz[:, sl],
                                         in1=B_acc[:, sl])
                    eng.drain()
                ins.then_inc(P.sems["tick"], 1)
            p.seg("vector", provides={"tick": 1})(sp_v)

            def d_store(eng, P, ch=ch):
                for ti in range(KI):
                    _dma(eng, P, "dms",
                         delta_o[ti * 128:(ti + 1) * 128, ch * CH:(ch + 1) * CH],
                         C_dp[:, ti * CH:(ti + 1) * CH])
            p.seg("sync", provides={"dms": 16 * KI})(d_store)

        p.build(ctx)
    return nc


def _w_image(W):
    K, M = W.shape
    a = W.reshape(K // 128, 128, M // 128, 128)
    return np.ascontiguousarray(
        a.transpose(1, 2, 0, 3).reshape(128, (K // 128) * (M // 128) * 128))


def _pack_v(v):
    n = v.shape[0]
    return np.ascontiguousarray(v.reshape(n // 128, 128).T)


def _l1_inmaps(inputs, x_cm):
    Win = _w_image(inputs["L0_attn_W_in"])
    Wdt = _w_image(inputs["L0_attn_W_dt"])
    Wxp = np.ascontiguousarray(
        inputs["L0_attn_W_xp"].reshape(KI, 128, 32).transpose(1, 0, 2)
        .reshape(128, KI * 32))
    b_in = inputs["L0_attn_b_in"]
    conv_w = inputs["L0_attn_conv_w"][:, 0, :]
    shared = {
        "Win_t": Win, "Wdt_t": Wdt, "Wxp_t": Wxp,
        "b_in_x": _pack_v(b_in[:DIN]),
        "b_in_z": _pack_v(b_in[DIN:]),
        "conv_w_p": np.ascontiguousarray(
            conv_w.reshape(KI, 128, 4).transpose(1, 0, 2).reshape(128, KI * 4)),
        "conv_b_p": _pack_v(inputs["L0_attn_conv_b"]),
        "b_dt_p": _pack_v(inputs["L0_attn_b_dt"]),
        "b_xp_p": np.ascontiguousarray(
            np.pad(inputs["L0_attn_b_xp"], (0, 96)).reshape(1, 128).T),
    }
    maps = []
    for c in range(NCORE):
        b, h = c // 2, c % 2
        t0 = h * TPC
        halo = np.zeros((D, 4), np.float32)
        if h == 1:
            halo[:, 0:3] = x_cm[b][:, t0 - 3:t0]
        m = dict(shared)
        m.update({
            "x_part": np.ascontiguousarray(x_cm[b][:, t0:t0 + TPC]),
            "x_halo": halo,
            "halo_mask": np.full((128, 1), float(h), np.float32),
        })
        maps.append(m)
    return maps


_L1_NC = None


def run_l1(inputs, x_cm):
    global _L1_NC
    if _L1_NC is None:
        _L1_NC = build_l1()
    res = run_bass_kernel_spmd(_L1_NC, _l1_inmaps(inputs, x_cm),
                               core_ids=list(range(NCORE)))
    return res.results




TCH = 512
NTCH = S // TCH
DTL = 8


def build_l2():
    nc = bass.Bass("TRN2", target_bir_lowering=False)
    delta_p = nc.dram_tensor("delta_p", [1024, S], F32, kind="ExternalInput").ap()
    xconv_p = nc.dram_tensor("xconv_p", [1024, S], BF16, kind="ExternalInput").ap()
    bc_p = nc.dram_tensor("bc_p", [32, S], F32, kind="ExternalInput").ap()
    A_p = nc.dram_tensor("A_p", [128, DTL * DST], F32, kind="ExternalInput").ap()
    dsk_p = nc.dram_tensor("dsk_p", [128, DTL], F32, kind="ExternalInput").ap()
    y_o = nc.dram_tensor("y_o", [1024, S], F32, kind="ExternalOutput").ap()

    with ExitStack() as ctx:
        E = ctx.enter_context
        p = Prog(nc, ctx)
        sb = lambda nm, sh, dt=F32: E(nc.sbuf_tensor(nm, sh, dt))

        d_t = [sb(f"d_t{i}", [128, TCH]) for i in range(2)]
        xc_t = [sb(f"xc_t{i}", [128, TCH], BF16) for i in range(2)]
        dX = sb("dX", [128, TCH])
        dA_all = sb("dA_all", [128, DST * TCH], BF16)
        dbx_all = sb("dbx_all", [128, DST * TCH], BF16)
        hbuf = sb("hbuf", [128, DST * TCH], BF16)
        ybuf = sb("ybuf", [128, DST * TCH], BF16)
        B_rep = sb("B_rep", [128, DST * TCH], BF16)
        C_rep = sb("C_rep", [128, DST * TCH], BF16)
        bc_row = sb("bc_row", [1, 32 * TCH])
        carry = sb("carry", [128, DTL * DST])
        yf = sb("yf", [128, TCH])
        A_sb = sb("A_sb", [128, DTL * DST])
        A_n = sb("A_n", [128, DTL * DST])
        dsk = sb("dsk", [128, DTL])
        ones_row = sb("ones_rowL2", [1, 128])

        ps_r = [E(nc.psum_tensor(f"psr{i}", [128, TCH], F32)) for i in range(2)]

        def init_ld(eng, P):
            _dma(eng, P, "dms", A_sb[:], A_p)
            _dma(eng, P, "dms", dsk[:], dsk_p)
        p.seg("sync", provides={"dms": 32})(init_ld)

        def init_v(eng, P):
            eng.memset(ones_row[:], 1.0)
            eng.memset(carry[:], 0.0).then_inc(P.sems["tick"], 1)
        p.seg("vector", provides={"tick": 1})(init_v)

        def init_s(eng, P):
            eng.activation(out=A_n[:], in_=A_sb[:],
                           func=AF.Exp).then_inc(P.sems["tick"], 1)
        p.seg("scalar", provides={"tick": 1})(init_s)

        def init_v2(eng, P):
            eng.tensor_scalar_mul(out=A_sb[:], in0=A_n[:],
                                  scalar1=-1.0).then_inc(P.sems["tick"], 1)
        p.seg("vector", provides={"tick": 1})(init_v2)

        for tch in range(NTCH):
            tsl = slice(tch * TCH, (tch + 1) * TCH)

            def bc_ld(eng, P, tch=tch):
                _dma(eng, P, "dms", bc_row[:],
                     bc_p[:, tch * TCH:(tch + 1) * TCH])
            p.seg("sync", provides={"dms": 16})(bc_ld)

            mm0, ev0 = p.base("mm"), p.base("ev")
            w0 = dict(p.tot)

            def rep_t(eng, P, mm0=mm0, ev0=ev0):
                for s in range(32):
                    if s >= 2:
                        eng.wait_ge(P.sems["ev"], ev0 + s - 1)
                    eng.matmul(ps_r[s % 2][:], ones_row[0:1, :],
                               bc_row[0:1, s * TCH:(s + 1) * TCH],
                               start=True, stop=True).then_inc(P.sems["mm"], 1)
            p.seg("tensor", provides={"mm": 32}, waits=w0)(rep_t)

            def rep_s(eng, P, mm0=mm0):
                for s in range(32):
                    eng.wait_ge(P.sems["mm"], mm0 + s + 1)
                    if s < DST:
                        out = B_rep[:, s * TCH:(s + 1) * TCH]
                    else:
                        out = C_rep[:, (s - DST) * TCH:(s - DST + 1) * TCH]
                    eng.copy(out=out, in_=ps_r[s % 2][:]).then_inc(P.sems["ev"], 1)
            p.seg("scalar", provides={"ev": 32}, waits=w0)(rep_s)

            for dt_i in range(DTL):
                buf = dt_i % 2
                if tch == 0 and dt_i == 0:
                    def ld0(eng, P):
                        _dma(eng, P, "dwa", d_t[0][:], delta_p[0:128, 0:TCH])
                    p.seg("sync", provides={"dwa": 16})(ld0)

                    def ld0g(eng, P):
                        _dma(eng, P, "dxa", xc_t[0][:], xconv_p[0:128, 0:TCH])
                    p.seg("gpsimd", provides={"dxa": 16})(ld0g)

                nxt = (tch * DTL + dt_i + 1)
                if nxt < NTCH * DTL:
                    ntch, ndt = nxt // DTL, nxt % DTL
                    w1 = dict(p.tot)

                    def ldn(eng, P, ntch=ntch, ndt=ndt, nb=(nxt % 2)):
                        _dma(eng, P, ("dwa", "dwb")[nb], d_t[nb][:],
                             delta_p[ndt * 128:(ndt + 1) * 128,
                                     ntch * TCH:(ntch + 1) * TCH])
                    p.seg("sync", provides={("dwa", "dwb")[nxt % 2]: 16},
                          waits=w1)(ldn)

                    def ldng(eng, P, ntch=ntch, ndt=ndt, nb=(nxt % 2)):
                        _dma(eng, P, ("dxa", "dxb")[nb], xc_t[nb][:],
                             xconv_p[ndt * 128:(ndt + 1) * 128,
                                     ntch * TCH:(ntch + 1) * TCH])
                    p.seg("gpsimd", provides={("dxa", "dxb")[nxt % 2]: 16},
                          waits=w1)(ldng)
                else:
                    w1 = dict(p.tot)

                def v1(eng, P, buf=buf):
                    eng.tensor_mul(out=dX[:], in0=d_t[buf][:],
                                   in1=xc_t[buf][:]).then_inc(P.sems["tick"], 1)
                p.seg("vector", provides={"tick": 1}, waits=w1)(v1)
                w2 = dict(p.tot)

                def s1(eng, P, buf=buf, dt_i=dt_i):
                    for s in range(DST):
                        ins = eng.activation(
                            out=dA_all[:, s * TCH:(s + 1) * TCH],
                            in_=d_t[buf][:], func=AF.Exp,
                            scale=A_sb[:, dt_i * DST + s:dt_i * DST + s + 1])
                    ins.then_inc(P.sems["tick"], 1)
                p.seg("scalar", provides={"tick": 1}, waits=w1)(s1)

                def g1(eng, P):
                    for s in range(DST):
                        ins = eng.tensor_mul(
                            out=dbx_all[:, s * TCH:(s + 1) * TCH],
                            in0=dX[:], in1=B_rep[:, s * TCH:(s + 1) * TCH])
                    ins.then_inc(P.sems["tick"], 1)
                p.seg("gpsimd", provides={"tick": 1}, waits=w2)(g1)

                def v2(eng, P, dt_i=dt_i, tch=tch):
                    for s in range(DST):
                        eng.tensor_tensor_scan(
                            out=hbuf[:, s * TCH:(s + 1) * TCH],
                            data0=dA_all[:, s * TCH:(s + 1) * TCH],
                            data1=dbx_all[:, s * TCH:(s + 1) * TCH],
                            initial=carry[:, dt_i * DST + s:dt_i * DST + s + 1],
                            op0=ALU.mult, op1=ALU.add)
                    eng.drain()
                    eng.tensor_copy(
                        out=carry[:, dt_i * DST:(dt_i + 1) * DST].rearrange(
                            "p (s o) -> p s o", o=1),
                        in_=hbuf[:].rearrange("p (s t) -> p s t", t=TCH)[
                            :, :, TCH - 1:TCH])
                    eng.drain()
                    eng.tensor_mul(out=ybuf[:], in0=hbuf[:], in1=C_rep[:])
                    eng.drain()
                    eng.tensor_reduce(
                        out=yf[:].rearrange("p (t o) -> p t o", o=1),
                        in_=ybuf[:].rearrange("p (s t) -> p t s", s=DST),
                        axis=AX.X, op=ALU.add)
                    eng.drain()
                    ins = eng.scalar_tensor_tensor(
                        out=dX[:], in0=xc_t[dt_i % 2][:],
                        scalar=dsk[:, dt_i:dt_i + 1], in1=yf[:],
                        op0=ALU.mult, op1=ALU.add)
                    eng.drain()
                    ins.then_inc(P.sems["tick"], 1)
                p.seg("vector", provides={"tick": 1})(v2)

                def st(eng, P, dt_i=dt_i, tch=tch):
                    _dma(eng, P, "dms",
                         y_o[dt_i * 128:(dt_i + 1) * 128,
                             tch * TCH:(tch + 1) * TCH], dX[:])
                p.seg("sync", provides={"dms": 16})(st)

        p.build(ctx)
    return nc


def _l2_inmaps(inputs, DELTA, XCONV, BCF):
    A_log = inputs["L0_attn_A_log"]
    D_skip = inputs["L0_attn_D_skip"]
    maps = []
    for c in range(NCORE):
        b, dh = c // 2, c % 2
        rsl = slice(dh * 1024, (dh + 1) * 1024)
        A_sl = A_log[rsl].reshape(DTL, 128, DST).transpose(1, 0, 2).reshape(
            128, DTL * DST)
        dsk_sl = D_skip[rsl].reshape(DTL, 128).T
        maps.append({
            "delta_p": np.ascontiguousarray(DELTA[b][rsl]),
            "xconv_p": np.ascontiguousarray(XCONV[b][rsl]),
            "bc_p": np.ascontiguousarray(BCF[b]),
            "A_p": np.ascontiguousarray(A_sl),
            "dsk_p": np.ascontiguousarray(dsk_sl),
        })
    return maps


_L2_NC = None


def run_l2(inputs, DELTA, XCONV, BCF):
    global _L2_NC
    if _L2_NC is None:
        _L2_NC = build_l2()
    res = run_bass_kernel_spmd(_L2_NC, _l2_inmaps(inputs, DELTA, XCONV, BCF),
                               core_ids=list(range(NCORE)))
    return res.results




def build_l3():
    nc = bass.Bass("TRN2", target_bir_lowering=False)
    x_part = nc.dram_tensor("x_part", [D, TPC], F32, kind="ExternalInput").ap()
    y_part = nc.dram_tensor("y_part", [DIN, TPC], F32, kind="ExternalInput").ap()
    sz_part = nc.dram_tensor("sz_part", [DIN, TPC], BF16, kind="ExternalInput").ap()
    Wout_t = nc.dram_tensor("Wout_t", [128, 8 * KI * 128], F32, kind="ExternalInput").ap()
    WT_t = nc.dram_tensor("WT_t", [128, KD * KD * 128], F32, kind="ExternalInput").ap()
    M0_t = nc.dram_tensor("M0_t", [128, KD * 64], F32, kind="ExternalInput").ap()
    dsh_t = nc.dram_tensor("dsh_t", [64, KD * 128], F32, kind="ExternalInput").ap()
    b_out_p = nc.dram_tensor("b_out_p", [128, KD], F32, kind="ExternalInput").ap()
    alpha_p = nc.dram_tensor("alpha_p", [128, KD], F32, kind="ExternalInput").ap()
    beta_p = nc.dram_tensor("beta_p", [128, KD], F32, kind="ExternalInput").ap()
    x1_o = nc.dram_tensor("x1_o", [D, TPC], F32, kind="ExternalOutput").ap()

    with ExitStack() as ctx:
        E = ctx.enter_context
        p = Prog(nc, ctx)
        sb = lambda nm, sh, dt=F32: E(nc.sbuf_tensor(nm, sh, dt))

        wout = sb("wout", [128, 8 * KI * 128], BF16)
        wt = sb("wt", [128, KD * KD * 128], BF16)
        m0 = sb("m0", [128, KD * 64], BF16)
        dsh = sb("dsh", [64, KD * 128], BF16)
        x_f = sb("x_f", [128, KD * CH])
        x_bf = sb("x_bf", [128, KD * CH], BF16)
        xsq = sb("xsq", [128, KD * CH], BF16)
        xn_bf = sb("xn_bf", [128, KD * CH], BF16)
        ybuf = sb("ybuf", [128, KI * CH])
        szb = sb("szb", [128, KI * CH], BF16)
        y2_bf = sb("y2_bf", [128, KI * CH], BF16)
        r_bf = sb("r_bf", [128, KD * CH], BF16)
        f_out = sb("f_out", [128, KD * CH])
        pool_bf = sb("pool_bf", [64, CH], BF16)
        vb = {nm: sb("v_" + nm, [128, KD]) for nm in ("bout", "alpha", "beta")}
        ones_col = sb("ones_colL3", [128, 1], BF16)
        ones_row = sb("ones_rowL3", [1, 128], F32)
        cst = sb("cstL3", [1, 4], F32)
        rows = {nm: sb("rowL3_" + nm, [1, CH]) for nm in
                ("mu", "ex2", "var", "sd", "r", "mur")}
        tmp_t = sb("tmpL3", [128, CH], F32)
        tmp2_t = sb("tmp2L3", [128, CH], F32)

        ps = [E(nc.psum_tensor(f"psL3{i}", [128, CH], F32)) for i in range(4)]
        ps_s1 = E(nc.psum_tensor("psL3_s1", [128, CH], F32))
        ps_s2 = E(nc.psum_tensor("psL3_s2", [128, CH], F32))
        ps_rR = E(nc.psum_tensor("psL3_rR", [128, CH], F32))
        ps_rM = E(nc.psum_tensor("psL3_rM", [128, CH], F32))

        def init_ld(eng, P):
            for nm, srcv in (("bout", b_out_p), ("alpha", alpha_p),
                             ("beta", beta_p)):
                _dma(eng, P, "dms", vb[nm][:], srcv)
        p.seg("sync", provides={"dms": 48})(init_ld)

        def init_w(eng, P):
            _dma(eng, P, "dmw", wout[:], Wout_t)
            _dma(eng, P, "dmw", wt[:], WT_t)
            _dma(eng, P, "dmw", m0[:], M0_t)
            _dma(eng, P, "dmw", dsh[:], dsh_t)
        p.seg("gpsimd", provides={"dmw": 64})(init_w)

        def init_v(eng, P):
            eng.memset(ones_col[:], 1.0)
            eng.memset(ones_row[:], 1.0)
            eng.memset(cst[0:1, 0:1], 1e-5).then_inc(P.sems["tick"], 1)
        p.seg("vector", provides={"tick": 1})(init_v)

        for ch in range(NCH):
            csl = slice(ch * CH, (ch + 1) * CH)

            def ld_f(eng, P, ch=ch):
                for ki in range(KD):
                    _dma(eng, P, "dms", x_f[:, ki * CH:(ki + 1) * CH],
                         x_part[ki * 128:(ki + 1) * 128, ch * CH:(ch + 1) * CH])
                for ti in range(KI):
                    _dma(eng, P, "dms", ybuf[:, ti * CH:(ti + 1) * CH],
                         y_part[ti * 128:(ti + 1) * 128, ch * CH:(ch + 1) * CH])
            p.seg("sync", provides={"dms": 16 * (KD + KI)})(ld_f)

            def ld_g(eng, P, ch=ch):
                for ki in range(KD):
                    _dma(eng, P, "dmg", x_bf[:, ki * CH:(ki + 1) * CH],
                         x_part[ki * 128:(ki + 1) * 128, ch * CH:(ch + 1) * CH])
                for ti in range(KI):
                    _dma(eng, P, "dmg", szb[:, ti * CH:(ti + 1) * CH],
                         sz_part[ti * 128:(ti + 1) * 128, ch * CH:(ch + 1) * CH])
            p.seg("gpsimd", provides={"dmg": 16 * (KD + KI)})(ld_g)

            def v_y2(eng, P):
                for ti in range(KI):
                    sl = slice(ti * CH, (ti + 1) * CH)
                    ins = eng.tensor_mul(out=y2_bf[:, sl], in0=ybuf[:, sl],
                                         in1=szb[:, sl])
                ins.then_inc(P.sems["tick"], 1)
            p.seg("vector", provides={"tick": 1})(v_y2)

            emit_ln(p, KD, CH,
                    xbf=lambda ki: x_bf[:, ki * CH:(ki + 1) * CH],
                    sq_buf=lambda ki: xsq[:, ki * CH:(ki + 1) * CH],
                    rows=rows, ps_s1=ps_s1, ps_s2=ps_s2, ps_rR=ps_rR, ps_rM=ps_rM,
                    write=lambda ki, eng, t, psM: eng.tensor_sub(
                        out=xn_bf[:, ki * CH:(ki + 1) * CH], in0=t, in1=psM),
                    tmp=lambda ki: tmp_t[:, 0:CH],
                    ones_col=ones_col, ones_row=ones_row, mode="plain",
                    eps_ap=cst[0:1, 0:1])

            emit_res_mm(
                p, wsb=lambda ki, mi: wout[:, (mi * KI + ki) * 128:
                                           (mi * KI + ki + 1) * 128],
                Mt=KD, Kt=KI, N=CH,
                rhs=lambda ki: y2_bf[:, ki * CH:(ki + 1) * CH],
                evac=lambda mi, eng, bank: eng.activation(
                    out=ybuf[:, mi * CH:(mi + 1) * CH], in_=bank[:, 0:CH],
                    func=AF.Identity, bias=vb["bout"][:, mi:mi + 1]),
                ps=ps)

            def v_r(eng, P):
                for ki in range(KD):
                    sl = slice(ki * CH, (ki + 1) * CH)
                    ins = eng.tensor_add(out=r_bf[:, sl], in0=xn_bf[:, sl],
                                         in1=ybuf[:, sl])
                ins.then_inc(P.sems["tick"], 1)
            p.seg("vector", provides={"tick": 1})(v_r)

            def wr_fin(ki, eng, t, psM):
                eng.tensor_sub(out=tmp2_t[:], in0=t, in1=psM)
                eng.drain()
                ins = eng.tensor_scalar(
                    out=ybuf[:, (KD + ki) * CH:(KD + ki + 1) * CH],
                    in0=tmp2_t[:], scalar1=vb["alpha"][:, ki:ki + 1],
                    scalar2=vb["beta"][:, ki:ki + 1],
                    op0=ALU.mult, op1=ALU.add)
                eng.drain()
                return ins
            emit_ln(p, KD, CH,
                    xbf=lambda ki: r_bf[:, ki * CH:(ki + 1) * CH],
                    sq_buf=lambda ki: xsq[:, ki * CH:(ki + 1) * CH],
                    rows=rows, ps_s1=ps_s1, ps_s2=ps_s2, ps_rR=ps_rR, ps_rM=ps_rM,
                    write=wr_fin, tmp=lambda ki: tmp_t[:, 0:CH],
                    ones_col=ones_col, ones_row=ones_row, mode="l0final",
                    eps=1e-6)

            def pool_t(eng, P):
                for ki in range(KD):
                    ins = eng.matmul(ps_s1[0:64, 0:CH],
                                     m0[:, ki * 64:(ki + 1) * 64],
                                     xn_bf[:, ki * CH:(ki + 1) * CH],
                                     start=(ki == 0), stop=(ki == KD - 1))
                ins.then_inc(P.sems["mm"], 1)
            p.seg("tensor", provides={"mm": 1})(pool_t)

            def pool_ev(eng, P):
                eng.copy(out=pool_bf[:], in_=ps_s1[0:64, 0:CH]).then_inc(
                    P.sems["ev"], 1)
            p.seg("scalar", provides={"ev": 1})(pool_ev)

            def f_extra(mi, eng, bank):
                return eng.matmul(bank[:, 0:CH],
                                  dsh[:, mi * 128:(mi + 1) * 128],
                                  pool_bf[:], start=False, stop=True)
            emit_res_mm(
                p, wsb=lambda ki, mi: wt[:, (mi * KD + ki) * 128:
                                         (mi * KD + ki + 1) * 128],
                Mt=KD, Kt=KD, N=CH,
                rhs=lambda ki: xn_bf[:, ki * CH:(ki + 1) * CH],
                evac=lambda mi, eng, bank: eng.copy(
                    out=f_out[:, mi * CH:(mi + 1) * CH], in_=bank[:, 0:CH]),
                ps=ps, mm_extra=f_extra)

            def v_x1(eng, P):
                for ki in range(KD):
                    sl = slice(ki * CH, (ki + 1) * CH)
                    eng.tensor_add(out=tmp_t[:], in0=x_f[:, sl],
                                   in1=ybuf[:, (KD + ki) * CH:(KD + ki + 1) * CH])
                    eng.drain()
                    ins = eng.tensor_add(out=ybuf[:, sl], in0=tmp_t[:],
                                         in1=f_out[:, sl])
                    eng.drain()
                ins.then_inc(P.sems["tick"], 1)
            p.seg("vector", provides={"tick": 1})(v_x1)

            def st_x1(eng, P, ch=ch):
                for ki in range(KD):
                    _dma(eng, P, "dms",
                         x1_o[ki * 128:(ki + 1) * 128, ch * CH:(ch + 1) * CH],
                         ybuf[:, ki * CH:(ki + 1) * CH])
            p.seg("sync", provides={"dms": 16 * KD})(st_x1)

        p.build(ctx)
    return nc


def _l3_inmaps(inputs, x_cm, Y, SZ):
    Wout = _w_image(inputs["L0_attn_W_out"])
    WT = _w_image(np.ascontiguousarray(inputs["L0_ffn_weight"].T))
    M0 = (np.arange(1024)[:, None] % 64 == np.arange(64)[None, :]).astype(np.float32)
    M0i = np.ascontiguousarray(
        M0.reshape(KD, 128, 64).transpose(1, 0, 2).reshape(128, KD * 64))
    dsh = np.ascontiguousarray(inputs["L0_ffn_disha"])
    shared = {
        "Wout_t": Wout, "WT_t": WT, "M0_t": M0i, "dsh_t": dsh,
        "b_out_p": _pack_v(inputs["L0_attn_b_out"]),
        "alpha_p": _pack_v(inputs["L0_attn_alpha"]),
        "beta_p": _pack_v(inputs["L0_attn_beta"]),
    }
    maps = []
    for c in range(NCORE):
        b, h = c // 2, c % 2
        t0 = h * TPC
        m = dict(shared)
        m.update({
            "x_part": np.ascontiguousarray(x_cm[b][:, t0:t0 + TPC]),
            "y_part": np.ascontiguousarray(Y[b][:, t0:t0 + TPC]),
            "sz_part": np.ascontiguousarray(SZ[b][:, t0:t0 + TPC]),
        })
        maps.append(m)
    return maps


_L3_NC = None


def run_l3(inputs, x_cm, Y, SZ):
    global _L3_NC
    if _L3_NC is None:
        _L3_NC = build_l3()
    res = run_bass_kernel_spmd(_L3_NC, _l3_inmaps(inputs, x_cm, Y, SZ),
                               core_ids=list(range(NCORE)))
    return res.results




KH = 32


def build_l4():
    nc = bass.Bass("TRN2", target_bir_lowering=False)
    x1_p = nc.dram_tensor("x1_p", [D, TPC], F32, kind="ExternalInput").ap()
    projT_t = nc.dram_tensor("projT_t", [128, KD * KD * 128], F32, kind="ExternalInput").ap()
    fc1T_t = nc.dram_tensor("fc1T_t", [128, KH * KD * 128], F32, kind="ExternalInput").ap()
    fc2T_t = nc.dram_tensor("fc2T_t", [128, KD * KH * 128], F32, kind="ExternalInput").ap()
    w1T_t = nc.dram_tensor("w1T_t", [128, KH * KD * 128], F32, kind="ExternalInput").ap()
    w2T_t = nc.dram_tensor("w2T_t", [128, KD * KH * 128], F32, kind="ExternalInput").ap()
    wr_t = nc.dram_tensor("wr_t", [128, KD * 16], F32, kind="ExternalInput").ap()
    oh_t = nc.dram_tensor("oh_t", [16, 16 * 128], F32, kind="ExternalInput").ap()
    id_t = nc.dram_tensor("id_t", [128, 128], F32, kind="ExternalInput").ap()
    vec_names = ("proj_b", "ln1_g", "ln1_b", "ln2_g", "ln2_b", "fc2_b",
                 "b2", "g1", "g2")
    vins = {nm: nc.dram_tensor("i_" + nm, [128, KD], F32,
                               kind="ExternalInput").ap() for nm in vec_names}
    vec32 = ("lnh_g", "lnh_b", "fc1_b", "b1")
    vins32 = {nm: nc.dram_tensor("i_" + nm, [128, KH], F32,
                                 kind="ExternalInput").ap() for nm in vec32}
    brt = nc.dram_tensor("i_brt", [128, 1], F32, kind="ExternalInput").ap()
    x2_o = nc.dram_tensor("x2_o", [D, TPC], F32, kind="ExternalOutput").ap()

    with ExitStack() as ctx:
        E = ctx.enter_context
        p = Prog(nc, ctx)
        sb = lambda nm, sh, dt=F32: E(nc.sbuf_tensor(nm, sh, dt))

        projT = sb("projT", [128, KD * KD * 128], BF16)
        wr = sb("wr", [128, KD * 16], BF16)
        oh = sb("oh", [16, 16 * 128])
        idm = sb("idm", [128, 128])
        wb = [sb(f"wbL4{i}", [128, 4 * KD * 128], BF16) for i in range(2)]
        x1f = sb("x1f", [128, KD * CH])
        x1bf = sb("x1bf", [128, KD * CH], BF16)
        xsq = sb("xsqL4", [128, KD * CH], BF16)
        xn1 = sb("xn1", [128, KD * CH], BF16)
        y1z2 = sb("y1z2", [128, KD * CH], BF16)
        p_bf = sb("p_bfL4", [128, KD * CH], BF16)
        t_bf = sb("t_bfL4", [128, KD * CH], BF16)
        hbb = sb("hbb", [128, KH * CH], BF16)
        sqp = sb("sqp", [128, 2 * CH], BF16)
        m_bf = sb("m_bfL4", [128, KD * CH], BF16)
        o_f = sb("o_fL4", [128, KD * CH])
        lgt = sb("lgt", [16, CH])
        lt = sb("lt", [128, DST])
        lta = sb("lta", [128, DST])
        ltb = sb("ltb", [128, DST])
        thrc = sb("thrc", [128, 4])
        thr_row = sb("thr_row", [1, CH])
        mask_cm = sb("mask_cm", [16, CH])
        mrow = sb("mrow", [128, 1])
        vb = {nm: sb("vL4_" + nm, [128, KD]) for nm in vec_names}
        vb32 = {nm: sb("vL4_" + nm, [128, KH]) for nm in vec32}
        brt_s = sb("brt_s", [128, 1])
        ones_col = sb("ones_colL4", [128, 1], BF16)
        ones_row = sb("ones_rowL4", [1, 128], F32)
        cst = sb("cstL4", [1, 4], F32)
        rows = {nm: sb("rowL4_" + nm, [1, CH]) for nm in
                ("mu", "ex2", "var", "sd", "r", "mur")}
        tmp_t = sb("tmpL4", [128, CH], F32)
        tmp2_t = sb("tmp2L4", [128, CH], F32)

        ps = [E(nc.psum_tensor(f"psL4{i}", [128, CH], F32)) for i in range(4)]
        ps_s1 = E(nc.psum_tensor("psL4_s1", [128, CH], F32))
        ps_s2 = E(nc.psum_tensor("psL4_s2", [128, CH], F32))
        ps_rR = E(nc.psum_tensor("psL4_rR", [128, CH], F32))
        ps_rM = E(nc.psum_tensor("psL4_rM", [128, CH], F32))

        def hm(ti):
            srcs = [t_bf, y1z2, x1bf, xsq]
            return srcs[ti // KD][:, (ti % KD) * CH:(ti % KD + 1) * CH]

        def init_ld(eng, P):
            for nm in vec_names:
                _dma(eng, P, "dms", vb[nm][:], vins[nm])
            for nm in vec32:
                _dma(eng, P, "dms", vb32[nm][:], vins32[nm])
            _dma(eng, P, "dms", brt_s[:], brt)
            _dma(eng, P, "dms", oh[:], oh_t)
            _dma(eng, P, "dms", idm[:], id_t)
        p.seg("sync", provides={"dms": 16 * (9 + 4 + 3)})(init_ld)

        def init_w(eng, P):
            _dma(eng, P, "dmw", projT[:], projT_t)
            _dma(eng, P, "dmw", wr[:], wr_t)
        p.seg("gpsimd", provides={"dmw": 32})(init_w)

        def init_v(eng, P):
            eng.memset(ones_col[:], 1.0)
            eng.memset(ones_row[:], 1.0)
            eng.memset(cst[0:1, 0:1], 1e-5).then_inc(P.sems["tick"], 1)
        p.seg("vector", provides={"tick": 1})(init_v)

        def aff_write(dst, g, b_):
            def wr(ki, eng, t, psM):
                eng.tensor_sub(out=tmp2_t[:], in0=t, in1=psM)
                eng.drain()
                ins = eng.tensor_scalar(
                    out=dst(ki), in0=tmp2_t[:], scalar1=g[:, ki:ki + 1],
                    scalar2=b_[:, ki:ki + 1], op0=ALU.mult, op1=ALU.add)
                eng.drain()
                return ins
            return wr

        for ch in range(NCH):
            def ld_f(eng, P, ch=ch):
                for ki in range(KD):
                    _dma(eng, P, "dms", x1f[:, ki * CH:(ki + 1) * CH],
                         x1_p[ki * 128:(ki + 1) * 128, ch * CH:(ch + 1) * CH])
            p.seg("sync", provides={"dms": 16 * KD})(ld_f)

            def ld_g(eng, P, ch=ch):
                for ki in range(KD):
                    _dma(eng, P, "dmg", x1bf[:, ki * CH:(ki + 1) * CH],
                         x1_p[ki * 128:(ki + 1) * 128, ch * CH:(ch + 1) * CH])
            p.seg("gpsimd", provides={"dmg": 16 * KD})(ld_g)

            emit_ln(p, KD, CH,
                    xbf=lambda ki: x1bf[:, ki * CH:(ki + 1) * CH],
                    sq_buf=lambda ki: xsq[:, ki * CH:(ki + 1) * CH],
                    rows=rows, ps_s1=ps_s1, ps_s2=ps_s2, ps_rR=ps_rR, ps_rM=ps_rM,
                    write=lambda ki, eng, t, psM: eng.tensor_sub(
                        out=xn1[:, ki * CH:(ki + 1) * CH], in0=t, in1=psM),
                    tmp=lambda ki: tmp_t[:, 0:CH],
                    ones_col=ones_col, ones_row=ones_row, mode="plain",
                    eps_ap=cst[0:1, 0:1])

            emit_ln(p, KD, CH,
                    xbf=lambda ki: xn1[:, ki * CH:(ki + 1) * CH],
                    sq_buf=lambda ki: xsq[:, ki * CH:(ki + 1) * CH],
                    rows=rows, ps_s1=ps_s1, ps_s2=ps_s2, ps_rR=ps_rR, ps_rM=ps_rM,
                    write=aff_write(
                        lambda ki: y1z2[:, ki * CH:(ki + 1) * CH],
                        vb["ln1_g"], vb["ln1_b"]),
                    tmp=lambda ki: tmp_t[:, 0:CH],
                    ones_col=ones_col, ones_row=ones_row, mode="plain",
                    eps_ap=cst[0:1, 0:1])

            emit_res_mm(
                p, wsb=lambda ki, mi: projT[:, (mi * KD + ki) * 128:
                                            (mi * KD + ki + 1) * 128],
                Mt=KD, Kt=KD, N=CH,
                rhs=lambda ki: y1z2[:, ki * CH:(ki + 1) * CH],
                evac=lambda mi, eng, bank: eng.activation(
                    out=p_bf[:, mi * CH:(mi + 1) * CH], in_=bank[:, 0:CH],
                    func=AF.Identity, bias=vb["proj_b"][:, mi:mi + 1]),
                ps=ps)

            def v_t(eng, P):
                for ki in range(KD):
                    sl = slice(ki * CH, (ki + 1) * CH)
                    ins = eng.scalar_tensor_tensor(
                        out=t_bf[:, sl], in0=p_bf[:, sl],
                        scalar=vb["g1"][:, ki:ki + 1], in1=xn1[:, sl],
                        op0=ALU.mult, op1=ALU.add)
                ins.then_inc(P.sems["tick"], 1)
            p.seg("vector", provides={"tick": 1})(v_t)

            emit_ln(p, KD, CH,
                    xbf=lambda ki: t_bf[:, ki * CH:(ki + 1) * CH],
                    sq_buf=lambda ki: xsq[:, ki * CH:(ki + 1) * CH],
                    rows=rows, ps_s1=ps_s1, ps_s2=ps_s2, ps_rR=ps_rR, ps_rM=ps_rM,
                    write=aff_write(
                        lambda ki: y1z2[:, ki * CH:(ki + 1) * CH],
                        vb["ln2_g"], vb["ln2_b"]),
                    tmp=lambda ki: tmp_t[:, 0:CH],
                    ones_col=ones_col, ones_row=ones_row, mode="plain",
                    eps_ap=cst[0:1, 0:1])

            emit_stream_mm(
                p, src=lambda g: fc1T_t[:, (g * 4) * KD * 128:
                                        (g + 1) * 4 * KD * 128],
                bufs=wb, Mt=KH, Kt=KD, N=CH,
                rhs=lambda ki: y1z2[:, ki * CH:(ki + 1) * CH],
                evac=lambda mi, eng, bank: eng.activation(
                    out=hbb[:, mi * CH:(mi + 1) * CH], in_=bank[:, 0:CH],
                    func=AF.Gelu, bias=vb32["fc1_b"][:, mi:mi + 1]),
                ps=ps, group_mt=4)

            emit_ln(p, KH, CH,
                    xbf=lambda ki: hbb[:, ki * CH:(ki + 1) * CH],
                    sq_buf=lambda ki: sqp[:, (ki % 2) * CH:(ki % 2 + 1) * CH],
                    rows=rows, ps_s1=ps_s1, ps_s2=ps_s2, ps_rR=ps_rR, ps_rM=ps_rM,
                    write=aff_write(
                        lambda ki: hbb[:, ki * CH:(ki + 1) * CH],
                        vb32["lnh_g"], vb32["lnh_b"]),
                    tmp=lambda ki: tmp_t[:, 0:CH],
                    ones_col=ones_col, ones_row=ones_row, mode="plain",
                    eps_ap=cst[0:1, 0:1], sq_ping=True)

            emit_stream_mm(
                p, src=lambda g: fc2T_t[:, g * KH * 128:(g + 1) * KH * 128],
                bufs=wb, Mt=KD, Kt=KH, N=CH,
                rhs=lambda ki: hbb[:, ki * CH:(ki + 1) * CH],
                evac=lambda mi, eng, bank: eng.activation(
                    out=m_bf[:, mi * CH:(mi + 1) * CH], in_=bank[:, 0:CH],
                    func=AF.Identity, bias=vb["fc2_b"][:, mi:mi + 1]),
                ps=ps, group_mt=1)

            def rt_t(eng, P):
                for ki in range(KD):
                    ins = eng.matmul(ps_s1[0:16, 0:CH],
                                     wr[:, ki * 16:(ki + 1) * 16],
                                     xn1[:, ki * CH:(ki + 1) * CH],
                                     start=(ki == 0), stop=(ki == KD - 1))
                ins.then_inc(P.sems["mm"], 1)
            p.seg("tensor", provides={"mm": 1})(rt_t)

            def rt_ev(eng, P):
                eng.activation(out=lgt[:], in_=ps_s1[0:16, 0:CH],
                               func=AF.Identity,
                               bias=brt_s[0:16, 0:1]).then_inc(P.sems["ev"], 1)
            p.seg("scalar", provides={"ev": 1})(rt_ev)

            for blk in range(4):
                def tr1(eng, P, blk=blk):
                    eng.transpose(ps_rR[:, 0:16], lgt[:, blk * 128:(blk + 1) * 128],
                                  idm[0:16, 0:16]).then_inc(P.sems["mm"], 1)
                p.seg("tensor", provides={"mm": 1})(tr1)

                def tr1e(eng, P):
                    eng.copy(out=lt[:, 0:DST],
                             in_=ps_rR[:, 0:16]).then_inc(P.sems["ev"], 1)
                p.seg("scalar", provides={"ev": 1})(tr1e)

                def top4(eng, P, blk=blk):
                    eng.reduce_max(out=thrc[:, blk:blk + 1], in_=lt[:, 0:DST],
                                   axis=AX.X)
                    eng.drain()
                    cur = lt
                    for rnd, nxt in ((1, lta), (2, ltb), (3, lta)):
                        if rnd == 3:
                            nxt = lt
                        eng.tensor_scalar(out=tmp_t[:, 0:DST], in0=cur[:, 0:DST],
                                          scalar1=thrc[:, blk:blk + 1],
                                          scalar2=None, op0=ALU.is_ge)
                        eng.drain()
                        eng.scalar_tensor_tensor(
                            out=nxt[:, 0:DST], in0=tmp_t[:, 0:DST],
                            scalar=-1e30, in1=cur[:, 0:DST],
                            op0=ALU.mult, op1=ALU.add)
                        eng.drain()
                        ins = eng.reduce_max(out=thrc[:, blk:blk + 1],
                                             in_=nxt[:, 0:DST], axis=AX.X)
                        eng.drain()
                        cur = nxt
                    ins.then_inc(P.sems["tick"], 1)
                p.seg("vector", provides={"tick": 1})(top4)

                def tr2(eng, P, blk=blk):
                    eng.transpose(ps_rM[0:1, 0:128], thrc[:, blk:blk + 1],
                                  idm[:, :]).then_inc(P.sems["mm"], 1)
                p.seg("tensor", provides={"mm": 1})(tr2)

                def tr2e(eng, P, blk=blk):
                    eng.copy(out=thr_row[0:1, blk * 128:(blk + 1) * 128],
                             in_=ps_rM[0:1, 0:128]).then_inc(P.sems["ev"], 1)
                p.seg("scalar", provides={"ev": 1})(tr2e)

            def thr_rep(eng, P):
                eng.matmul(ps_s2[0:16, 0:CH], ones_row[0:1, 0:16],
                           thr_row[0:1, 0:CH], start=True,
                           stop=True).then_inc(P.sems["mm"], 1)
            p.seg("tensor", provides={"mm": 1})(thr_rep)

            def v_mask(eng, P):
                eng.tensor_tensor(out=mask_cm[:], in0=lgt[:],
                                  in1=ps_s2[0:16, 0:CH],
                                  op=ALU.is_ge).then_inc(P.sems["tick"], 1)
            p.seg("vector", provides={"tick": 1})(v_mask)

            emit_stream_mm(
                p, src=lambda g: w1T_t[:, (g * 4) * KD * 128:
                                       (g + 1) * 4 * KD * 128],
                bufs=wb, Mt=KH, Kt=KD, N=CH,
                rhs=lambda ki: xn1[:, ki * CH:(ki + 1) * CH],
                evac=lambda mi, eng, bank: eng.activation(
                    out=hbb[:, mi * CH:(mi + 1) * CH], in_=bank[:, 0:CH],
                    func=AF.Relu, bias=vb32["b1"][:, mi:mi + 1]),
                ps=ps, group_mt=4)

            mm0, ev0 = p.base("mm"), p.base("ev")
            tk0 = p.tot["tk_vector"]
            w0 = dict(p.tot)

            def mrep_t(eng, P, mm0=mm0, tk0=tk0):
                for e in range(16):
                    if e >= 2:
                        eng.wait_ge(P.sems["tk_vector"], tk0 + e - 1)
                    eng.matmul(ps_rR[:, 0:CH] if e % 2 == 0 else ps_rM[:, 0:CH],
                               oh[:, e * 128:(e + 1) * 128],
                               mask_cm[:], start=True,
                               stop=True).then_inc(P.sems["mm"], 1)
            p.seg("tensor", provides={"mm": 16}, waits=w0)(mrep_t)

            def mmask_v(eng, P, mm0=mm0):
                for e in range(16):
                    eng.wait_ge(P.sems["mm"], mm0 + e + 1)
                    bank = ps_rR if e % 2 == 0 else ps_rM
                    for j in range(2):
                        ti = e * 2 + j
                        ins = eng.tensor_mul(
                            out=hm(ti), in0=hbb[:, ti * CH:(ti + 1) * CH],
                            in1=bank[:, 0:CH])
                    ins.then_inc(P.sems["tick"], 1)
            p.seg("vector", provides={"tick": 16}, waits=w0)(mmask_v)

            emit_stream_mm(
                p, src=lambda g: w2T_t[:, g * KH * 128:(g + 1) * KH * 128],
                bufs=wb, Mt=KD, Kt=KH, N=CH,
                rhs=lambda ki: hm(ki),
                evac=lambda mi, eng, bank: eng.activation(
                    out=o_f[:, mi * CH:(mi + 1) * CH], in_=bank[:, 0:CH],
                    func=AF.Identity, bias=vb["b2"][:, mi:mi + 1]),
                ps=ps, group_mt=1)

            def v_asm(eng, P):
                for ki in range(KD):
                    sl = slice(ki * CH, (ki + 1) * CH)
                    eng.tensor_add(out=tmp_t[:], in0=x1f[:, sl], in1=xn1[:, sl])
                    eng.drain()
                    eng.scalar_tensor_tensor(
                        out=tmp2_t[:], in0=p_bf[:, sl],
                        scalar=vb["g1"][:, ki:ki + 1], in1=tmp_t[:],
                        op0=ALU.mult, op1=ALU.add)
                    eng.drain()
                    eng.scalar_tensor_tensor(
                        out=tmp_t[:], in0=m_bf[:, sl],
                        scalar=vb["g2"][:, ki:ki + 1], in1=tmp2_t[:],
                        op0=ALU.mult, op1=ALU.add)
                    eng.drain()
                    ins = eng.tensor_add(out=x1f[:, sl], in0=tmp_t[:],
                                         in1=o_f[:, sl])
                    eng.drain()
                ins.then_inc(P.sems["tick"], 1)
            p.seg("vector", provides={"tick": 1})(v_asm)

            def st_x2(eng, P, ch=ch):
                for ki in range(KD):
                    _dma(eng, P, "dms",
                         x2_o[ki * 128:(ki + 1) * 128, ch * CH:(ch + 1) * CH],
                         x1f[:, ki * CH:(ki + 1) * CH])
            p.seg("sync", provides={"dms": 16 * KD})(st_x2)

        p.build(ctx)
    return nc


def _pack_v32(v):
    return np.ascontiguousarray(v.reshape(KH, 128).T)


def _l4_inmaps(inputs, X1):
    pref = "L1_attn_"
    shared = {
        "projT_t": _w_image(np.ascontiguousarray(inputs[pref + "proj_w"].T)),
        "fc1T_t": _w_image(np.ascontiguousarray(inputs[pref + "fc1_w"].T)),
        "fc2T_t": _w_image(np.ascontiguousarray(inputs[pref + "fc2_w"].T)),
        "w1T_t": _w_image(np.ascontiguousarray(inputs["L1_ffn_w1"].T)),
        "w2T_t": _w_image(np.ascontiguousarray(inputs["L1_ffn_w2"].T)),
        "wr_t": np.ascontiguousarray(
            inputs["L1_ffn_w_router"].reshape(KD, 128, 16).transpose(1, 0, 2)
            .reshape(128, KD * 16)),
        "oh_t": np.ascontiguousarray(
            (np.arange(16)[:, None] == np.repeat(np.arange(16), 128)[None, :])
            .astype(np.float32)),
        "id_t": np.eye(128, dtype=np.float32),
        "i_proj_b": _pack_v(inputs[pref + "proj_b"]),
        "i_ln1_g": _pack_v(inputs[pref + "ln1_g"]),
        "i_ln1_b": _pack_v(inputs[pref + "ln1_b"]),
        "i_ln2_g": _pack_v(inputs[pref + "ln2_g"]),
        "i_ln2_b": _pack_v(inputs[pref + "ln2_b"]),
        "i_fc2_b": _pack_v(inputs[pref + "fc2_b"]),
        "i_b2": _pack_v(inputs["L1_ffn_b2"]),
        "i_g1": _pack_v(inputs[pref + "gamma1"]),
        "i_g2": _pack_v(inputs[pref + "gamma2"]),
        "i_lnh_g": _pack_v32(inputs[pref + "lnh_g"]),
        "i_lnh_b": _pack_v32(inputs[pref + "lnh_b"]),
        "i_fc1_b": _pack_v32(inputs[pref + "fc1_b"]),
        "i_b1": _pack_v32(inputs["L1_ffn_b1"]),
        "i_brt": np.ascontiguousarray(
            np.pad(inputs["L1_ffn_b_router"], (0, 112)).reshape(1, 128).T),
    }
    maps = []
    for c in range(NCORE):
        b, h = c // 2, c % 2
        m = dict(shared)
        m["x1_p"] = np.ascontiguousarray(X1[b][:, h * TPC:(h + 1) * TPC])
        maps.append(m)
    return maps


_L4_NC = None


def run_l4(inputs, X1):
    global _L4_NC
    if _L4_NC is None:
        _L4_NC = build_l4()
    res = run_bass_kernel_spmd(_L4_NC, _l4_inmaps(inputs, X1),
                               core_ids=list(range(NCORE)))
    return res.results


def kernel(**inputs):
    x = np.asarray(inputs["L0_attn_x"], np.float32)
    x_cm = np.ascontiguousarray(x.transpose(0, 2, 1))

    r1 = run_l1(inputs, x_cm)
    DELTA = np.stack([np.concatenate(
        [np.asarray(r1[2 * b]["delta_o"]), np.asarray(r1[2 * b + 1]["delta_o"])],
        axis=1) for b in range(B)])
    XCONV = np.stack([np.concatenate(
        [np.asarray(r1[2 * b]["xconv_o"]), np.asarray(r1[2 * b + 1]["xconv_o"])],
        axis=1) for b in range(B)])
    SZ = np.stack([np.concatenate(
        [np.asarray(r1[2 * b]["siluz_o"]), np.asarray(r1[2 * b + 1]["siluz_o"])],
        axis=1) for b in range(B)])
    BCF = np.stack([np.concatenate(
        [np.asarray(r1[2 * b]["bc_o"]), np.asarray(r1[2 * b + 1]["bc_o"])],
        axis=1) for b in range(B)])

    r2 = run_l2(inputs, DELTA, XCONV, BCF)
    Y = np.stack([np.concatenate(
        [np.asarray(r2[2 * b]["y_o"]), np.asarray(r2[2 * b + 1]["y_o"])],
        axis=0) for b in range(B)])

    r3 = run_l3(inputs, x_cm, Y, SZ)
    X1 = np.stack([np.concatenate(
        [np.asarray(r3[2 * b]["x1_o"]), np.asarray(r3[2 * b + 1]["x1_o"])],
        axis=1) for b in range(B)])

    r4 = run_l4(inputs, X1)
    X2 = np.stack([np.concatenate(
        [np.asarray(r4[2 * b]["x2_o"]), np.asarray(r4[2 * b + 1]["x2_o"])],
        axis=1) for b in range(B)])
    return np.ascontiguousarray(X2.transpose(0, 2, 1)).astype(np.float32)


# version 03
# speedup vs baseline: 1.0143x; 1.0143x over previous
import numpy as np
from contextlib import ExitStack
import concourse.bass as bass
import concourse.mybir as mybir
from concourse.bass_utils import run_bass_kernel_spmd

F32 = mybir.dt.float32
BF16 = mybir.dt.bfloat16
AF = mybir.ActivationFunctionType
ALU = mybir.AluOpType
AX = mybir.AxisListType

NCORE = 8
B, S, D = 4, 4096, 1024
DIN, DST = 2048, 16
TPC = S // 2
CH = 512
NCH = TPC // CH
KD = D // 128
KI = DIN // 128


class Prog:
    SEMNAMES = ("tk_sync", "tk_vector", "tk_scalar", "tk_tensor", "tk_gpsimd",
                "dms", "dmg", "dmw", "dwa", "dwb", "dxa", "dxb", "mm", "ev")
    ENGINES = ("sync", "vector", "scalar", "tensor", "gpsimd")

    def __init__(self, nc, ctx):
        self.nc = nc
        self.sems = {n: ctx.enter_context(nc.semaphore(n)) for n in self.SEMNAMES}
        self.tot = {n: 0 for n in self.SEMNAMES}
        self.segs = []

    def seg(self, engine, provides=None, waits=None):
        def deco(fn):
            self.segs.append((engine, fn, dict(waits if waits is not None
                                               else self.tot)))
            for k, v in (provides or {}).items():
                if k == "tick":
                    k = "tk_" + engine
                self.tot[k] += v
            return fn
        return deco

    def base(self, sem):
        return self.tot[sem]

    def build(self, ctx):
        self.segs.append(("sync", lambda eng, P: None, dict(self.tot)))
        block = ctx.enter_context(self.nc.Block())
        for en in self.ENGINES:
            fns = [(fn, waits) for (e, fn, waits) in self.segs if e == en]
            if not fns:
                continue

            def body(eng, fns=fns, en=en):
                self.sems["tick"] = self.sems["tk_" + en]
                for fn, waits in fns:
                    for sname, v in waits.items():
                        if v > 0:
                            eng.wait_ge(self.sems[sname], v)
                    fn(eng, self)
            getattr(block, en)(body)


def _dma(eng, P, sem, out, in_):
    eng.dma_start(out=out, in_=in_).then_inc(P.sems[sem], 16)


def emit_stream_mm(p, src, bufs, Mt, Kt, N, rhs, evac, ps, group_mt,
                   nb=4, tile_cols=128):
    ngroups = (Mt + group_mt - 1) // group_mt
    mm0, ev0 = p.base("mm"), p.base("ev")
    dw0 = (p.base("dwa"), p.base("dwb"))
    waits0 = dict(p.tot)

    def gload(eng, P):
        for g in range(ngroups):
            if g >= 2:
                eng.wait_ge(P.sems["mm"], mm0 + (g - 1) * group_mt)
            _dma(eng, P, ("dwa", "dwb")[g % 2], bufs[g % 2][:], src(g))
    na = 16 * ((ngroups + 1) // 2)
    nb_ = 16 * (ngroups // 2)
    p.seg("gpsimd", provides={"dwa": na, "dwb": nb_}, waits=waits0)(gload)

    def tseg(eng, P):
        for g in range(ngroups):
            eng.wait_ge(P.sems[("dwa", "dwb")[g % 2]],
                        dw0[g % 2] + 16 * (g // 2 + 1))
            for mi_l in range(min(group_mt, Mt - g * group_mt)):
                mi = g * group_mt + mi_l
                if mi >= nb:
                    eng.wait_ge(P.sems["ev"], ev0 + mi - nb + 1)
                bank = ps[mi % nb]
                for ki in range(Kt):
                    ins = eng.matmul(
                        bank[:, 0:N],
                        bufs[g % 2][:, (mi_l * Kt + ki) * tile_cols:
                                    (mi_l * Kt + ki + 1) * tile_cols],
                        rhs(ki), start=(ki == 0), stop=(ki == Kt - 1))
                ins.then_inc(P.sems["mm"], 1)
    p.seg("tensor", provides={"mm": Mt}, waits=waits0)(tseg)

    def sseg(eng, P):
        for mi in range(Mt):
            eng.wait_ge(P.sems["mm"], mm0 + mi + 1)
            evac(mi, eng, ps[mi % nb]).then_inc(P.sems["ev"], 1)
    p.seg("scalar", provides={"ev": Mt}, waits=waits0)(sseg)


def emit_res_mm(p, wsb, Mt, Kt, N, rhs, evac, ps, nb=4, mm_extra=None):
    mm0, ev0 = p.base("mm"), p.base("ev")
    waits0 = dict(p.tot)

    def tseg(eng, P):
        for mi in range(Mt):
            if mi >= nb:
                eng.wait_ge(P.sems["ev"], ev0 + mi - nb + 1)
            bank = ps[mi % nb]
            for ki in range(Kt):
                ins = eng.matmul(bank[:, 0:N], wsb(ki, mi), rhs(ki),
                                 start=(ki == 0),
                                 stop=(ki == Kt - 1 and mm_extra is None))
            if mm_extra is not None:
                ins = mm_extra(mi, eng, bank)
            ins.then_inc(P.sems["mm"], 1)
    p.seg("tensor", provides={"mm": Mt}, waits=waits0)(tseg)

    def sseg(eng, P):
        for mi in range(Mt):
            eng.wait_ge(P.sems["mm"], mm0 + mi + 1)
            evac(mi, eng, ps[mi % nb]).then_inc(P.sems["ev"], 1)
    p.seg("scalar", provides={"ev": Mt}, waits=waits0)(sseg)


def emit_ln(p, nk, N, xbf, sq_buf, rows, ps_s1, ps_s2, ps_rR, ps_rM, write,
            tmp, ones_col, ones_row, mode="plain", eps=1e-5, eps_ap=None,
            sq_ping=False):
    DD = nk * 128
    r = lambda nm: rows[nm][0:1, 0:N]

    if not sq_ping:
        def squares(eng, P):
            for ki in range(nk):
                ins = eng.activation(out=sq_buf(ki), in_=xbf(ki), func=AF.Square)
            ins.then_inc(P.sems["tick"], 1)
        p.seg("scalar", provides={"tick": 1})(squares)

        def stats(eng, P):
            for ki in range(nk):
                eng.matmul(ps_s1[0:1, 0:N], ones_col[:, 0:1], xbf(ki),
                           start=(ki == 0), stop=(ki == nk - 1))
            for ki in range(nk):
                ins = eng.matmul(ps_s2[0:1, 0:N], ones_col[:, 0:1], sq_buf(ki),
                                 start=(ki == 0), stop=(ki == nk - 1))
            ins.then_inc(P.sems["mm"], 1)
        p.seg("tensor", provides={"mm": 1})(stats)
    else:
        mm0, ev0 = p.base("mm"), p.base("ev")
        w0 = dict(p.tot)

        def squares(eng, P):
            for ki in range(nk):
                if ki >= 2:
                    eng.wait_ge(P.sems["mm"], mm0 + ki - 1)
                eng.activation(out=sq_buf(ki), in_=xbf(ki),
                               func=AF.Square).then_inc(P.sems["ev"], 1)
        p.seg("scalar", provides={"ev": nk}, waits=w0)(squares)

        def stats(eng, P):
            for ki in range(nk):
                eng.matmul(ps_s1[0:1, 0:N], ones_col[:, 0:1], xbf(ki),
                           start=(ki == 0), stop=(ki == nk - 1))
            for ki in range(nk):
                eng.wait_ge(P.sems["ev"], ev0 + ki + 1)
                eng.matmul(ps_s2[0:1, 0:N], ones_col[:, 0:1], sq_buf(ki),
                           start=(ki == 0),
                           stop=(ki == nk - 1)).then_inc(P.sems["mm"], 1)
        p.seg("tensor", provides={"mm": nk}, waits=w0)(stats)

    def rows1(eng, P):
        eng.tensor_scalar_mul(out=r("mu"), in0=ps_s1[0:1, 0:N], scalar1=1.0 / DD)
        eng.drain()
        if mode == "l0final":
            eng.tensor_mul(out=r("sd"), in0=r("mu"), in1=ps_s1[0:1, 0:N])
            eng.drain()
            eng.tensor_sub(out=r("ex2"), in0=ps_s2[0:1, 0:N], in1=r("sd"))
            eng.drain()
            ins = eng.tensor_scalar_mul(out=r("var"), in0=r("ex2"),
                                        scalar1=1.0 / (DD - 1))
        else:
            eng.tensor_scalar_mul(out=r("ex2"), in0=ps_s2[0:1, 0:N],
                                  scalar1=1.0 / DD)
            eng.tensor_mul(out=r("sd"), in0=r("mu"), in1=r("mu"))
            eng.drain()
            ins = eng.tensor_sub(out=r("var"), in0=r("ex2"), in1=r("sd"))
        ins.then_inc(P.sems["tick"], 1)
    p.seg("vector", provides={"tick": 1})(rows1)

    def rows2(eng, P):
        if mode == "l0final":
            ins = eng.activation(out=r("sd"), in_=r("var"), func=AF.Sqrt)
        else:
            ins = eng.activation(out=r("sd"), in_=r("var"), func=AF.Sqrt,
                                 bias=eps_ap)
        ins.then_inc(P.sems["tick"], 1)
    p.seg("scalar", provides={"tick": 1})(rows2)

    def rows3(eng, P):
        if mode == "l0final":
            eng.tensor_scalar_add(out=r("ex2"), in0=r("sd"), scalar1=float(eps))
            eng.drain()
            eng.reciprocal(out=r("r"), in_=r("ex2"))
        else:
            eng.reciprocal(out=r("r"), in_=r("sd"))
        eng.drain()
        ins = eng.tensor_mul(out=r("mur"), in0=r("mu"), in1=r("r"))
        ins.then_inc(P.sems["tick"], 1)
    p.seg("vector", provides={"tick": 1})(rows3)

    def reps(eng, P):
        eng.matmul(ps_rR[:, 0:N], ones_row[0:1, :], r("r"), start=True, stop=True)
        ins = eng.matmul(ps_rM[:, 0:N], ones_row[0:1, :], r("mur"),
                         start=True, stop=True)
        ins.then_inc(P.sems["mm"], 1)
    p.seg("tensor", provides={"mm": 1})(reps)

    def norm(eng, P):
        for ki in range(nk):
            t = tmp(ki)
            eng.tensor_mul(out=t, in0=xbf(ki), in1=ps_rR[:, 0:N])
            eng.drain()
            ins = write(ki, eng, t, ps_rM[:, 0:N])
            eng.drain()
        ins.then_inc(P.sems["tick"], 1)
    p.seg("vector", provides={"tick": 1})(norm)


def build_l1():
    nc = bass.Bass("TRN2", target_bir_lowering=False)
    x_part = nc.dram_tensor("x_part", [D, TPC], F32, kind="ExternalInput").ap()
    x_halo = nc.dram_tensor("x_halo", [D, 4], F32, kind="ExternalInput").ap()
    halo_mask = nc.dram_tensor("halo_mask", [128, 1], F32, kind="ExternalInput").ap()
    Win_t = nc.dram_tensor("Win_t", [128, 32 * KD * 128], F32, kind="ExternalInput").ap()
    Wdt_t = nc.dram_tensor("Wdt_t", [128, KI * KI * 128], F32, kind="ExternalInput").ap()
    Wxp_t = nc.dram_tensor("Wxp_t", [128, KI * 32], F32, kind="ExternalInput").ap()
    b_in_x = nc.dram_tensor("b_in_x", [128, KI], F32, kind="ExternalInput").ap()
    b_in_z = nc.dram_tensor("b_in_z", [128, KI], F32, kind="ExternalInput").ap()
    conv_w_p = nc.dram_tensor("conv_w_p", [128, KI * 4], F32, kind="ExternalInput").ap()
    conv_b_p = nc.dram_tensor("conv_b_p", [128, KI], F32, kind="ExternalInput").ap()
    b_dt_p = nc.dram_tensor("b_dt_p", [128, KI], F32, kind="ExternalInput").ap()
    b_xp_p = nc.dram_tensor("b_xp_p", [128, 1], F32, kind="ExternalInput").ap()

    l4wf = nc.dram_tensor("l4wf", [128, 16384], F32, kind="ExternalInput").ap()
    l4wb = nc.dram_tensor("l4wb", [128, 16384], BF16, kind="ExternalOutput").ap()
    delta_o = nc.dram_tensor("delta_o", [DIN, TPC], F32, kind="ExternalOutput").ap()
    xconv_o = nc.dram_tensor("xconv_o", [DIN, TPC], BF16, kind="ExternalOutput").ap()
    siluz_o = nc.dram_tensor("siluz_o", [DIN, TPC], BF16, kind="ExternalOutput").ap()
    bc_o = nc.dram_tensor("bc_o", [32, TPC], F32, kind="ExternalOutput").ap()

    with ExitStack() as ctx:
        E = ctx.enter_context
        p = Prog(nc, ctx)
        sb = lambda nm, sh, dt=F32: E(nc.sbuf_tensor(nm, sh, dt))

        win_b = [sb(f"win{i}", [128, 4 * KD * 128], BF16) for i in range(2)]
        wdt_b = [sb(f"wdt{i}", [128, 2 * KI * 128], BF16) for i in range(2)]
        wxp = sb("wxp", [128, KI * 32], BF16)
        xin = sb("xin", [128, KI * (CH + 3)], BF16)
        x_bf = sb("x_bf", [128, KD * CH], BF16)
        xsq = sb("xsq", [128, KD * CH], BF16)
        xn_bf = sb("xn_bf", [128, KD * CH], BF16)
        A_sz = sb("A_sz", [128, KI * CH], F32)
        B_acc = sb("B_acc", [128, KI * CH], F32)
        B_bf = sb("B_bf", [128, KI * CH], BF16)
        C_dp = sb("C_dp", [128, KI * CH], F32)
        halo_st = sb("halo_st", [128, KI * 4], F32)
        xh_bf = sb("xh_bf", [128, KD * 4], BF16)
        xhsq = sb("xhsq", [128, KD * 4], BF16)
        xnh_bf = sb("xnh_bf", [128, KD * 4], BF16)
        hm_sb = sb("hm_sb", [128, 1], F32)
        vecs = {nm: sb(nm + "_s", [128, KI]) for nm in
                ("binx", "binz", "convb", "bdt")}
        convw = sb("convw_s", [128, KI * 4], F32)
        bxp = sb("bxp_s", [128, 1], F32)
        ones_col = sb("ones_col", [128, 1], BF16)
        ones_row = sb("ones_row", [1, 128], F32)
        cst = sb("cst", [1, 4], F32)
        rows = {nm: sb("row_" + nm, [1, CH]) for nm in
                ("mu", "ex2", "var", "sd", "r", "mur")}
        hrows = {nm: sb("hrow_" + nm, [1, 4]) for nm in
                 ("mu", "ex2", "var", "sd", "r", "mur")}
        tmp_t = sb("tmp_t", [128, CH], F32)
        tmp_h = sb("tmp_h", [128, 4], F32)

        ps = [E(nc.psum_tensor(f"ps{i}", [128, CH], F32)) for i in range(4)]
        ps_s1 = E(nc.psum_tensor("ps_s1", [128, CH], F32))
        ps_s2 = E(nc.psum_tensor("ps_s2", [128, CH], F32))
        ps_rR = E(nc.psum_tensor("ps_rR", [128, CH], F32))
        ps_rM = E(nc.psum_tensor("ps_rM", [128, CH], F32))

        def init_loads(eng, P):
            for nm, src in (("binx", b_in_x), ("binz", b_in_z),
                            ("convb", conv_b_p), ("bdt", b_dt_p)):
                _dma(eng, P, "dms", vecs[nm][:], src)
            _dma(eng, P, "dms", convw[:], conv_w_p)
            _dma(eng, P, "dms", bxp[:], b_xp_p)
            _dma(eng, P, "dms", hm_sb[:], halo_mask)
        p.seg("sync", provides={"dms": 16 * 7})(init_loads)

        def init_wxp(eng, P):
            _dma(eng, P, "dmw", wxp[:], Wxp_t)
            for ki in range(KD):
                _dma(eng, P, "dmg", xh_bf[:, ki * 4:(ki + 1) * 4],
                     x_halo[ki * 128:(ki + 1) * 128, :])
            for q in range(4):
                _dma(eng, P, "dmg", l4wb[:, q * 4096:(q + 1) * 4096],
                     l4wf[:, q * 4096:(q + 1) * 4096])
        p.seg("gpsimd", provides={"dmw": 16, "dmg": 16 * (KD + 4)})(init_wxp)

        def init_ones(eng, P):
            eng.memset(ones_col[:], 1.0)
            eng.memset(ones_row[:], 1.0)
            eng.memset(cst[0:1, 0:1], 1e-5).then_inc(P.sems["tick"], 1)
        p.seg("vector", provides={"tick": 1})(init_ones)

        emit_ln(p, KD, 4,
                xbf=lambda ki: xh_bf[:, ki * 4:(ki + 1) * 4],
                sq_buf=lambda ki: xhsq[:, ki * 4:(ki + 1) * 4],
                rows=hrows, ps_s1=ps_s1, ps_s2=ps_s2, ps_rR=ps_rR, ps_rM=ps_rM,
                write=lambda ki, eng, t, psM: eng.tensor_sub(
                    out=xnh_bf[:, ki * 4:(ki + 1) * 4], in0=t, in1=psM),
                tmp=lambda ki: tmp_h[:, 0:4],
                ones_col=ones_col, ones_row=ones_row, mode="plain",
                eps_ap=cst[0:1, 0:1])

        emit_stream_mm(
            p, src=lambda g: Win_t[:, (g * 4) * KD * 128:(g + 1) * 4 * KD * 128],
            bufs=win_b, Mt=KI, Kt=KD, N=4,
            rhs=lambda ki: xnh_bf[:, ki * 4:(ki + 1) * 4],
            evac=lambda mi, eng, bank: eng.activation(
                out=halo_st[:, mi * 4:(mi + 1) * 4], in_=bank[:, 0:4],
                func=AF.Identity, bias=vecs["binx"][:, mi:mi + 1]),
            ps=ps, group_mt=4)

        def halo_fix(eng, P):
            for ti in range(KI):
                ins = eng.tensor_scalar_mul(
                    out=xin[:, ti * (CH + 3):ti * (CH + 3) + 3],
                    in0=halo_st[:, ti * 4:ti * 4 + 3],
                    scalar1=hm_sb[:, 0:1])
            ins.then_inc(P.sems["tick"], 1)
        p.seg("vector", provides={"tick": 1})(halo_fix)

        for ch in range(NCH):
            def x_load(eng, P, ch=ch):
                for ki in range(KD):
                    _dma(eng, P, "dmg", x_bf[:, ki * CH:(ki + 1) * CH],
                         x_part[ki * 128:(ki + 1) * 128, ch * CH:(ch + 1) * CH])
            p.seg("gpsimd", provides={"dmg": 16 * KD})(x_load)

            emit_ln(p, KD, CH,
                    xbf=lambda ki: x_bf[:, ki * CH:(ki + 1) * CH],
                    sq_buf=lambda ki: xsq[:, ki * CH:(ki + 1) * CH],
                    rows=rows, ps_s1=ps_s1, ps_s2=ps_s2, ps_rR=ps_rR, ps_rM=ps_rM,
                    write=lambda ki, eng, t, psM: eng.tensor_sub(
                        out=xn_bf[:, ki * CH:(ki + 1) * CH], in0=t, in1=psM),
                    tmp=lambda ki: A_sz[:, 0:CH],
                    ones_col=ones_col, ones_row=ones_row, mode="plain",
                    eps_ap=cst[0:1, 0:1])

            def win_evac(mi, eng, bank):
                if mi < KI:
                    return eng.activation(
                        out=xin[:, mi * (CH + 3) + 3: mi * (CH + 3) + 3 + CH],
                        in_=bank[:, 0:CH], func=AF.Identity,
                        bias=vecs["binx"][:, mi:mi + 1])
                ti = mi - KI
                return eng.activation(
                    out=A_sz[:, ti * CH:(ti + 1) * CH],
                    in_=bank[:, 0:CH], func=AF.Silu,
                    bias=vecs["binz"][:, ti:ti + 1])

            emit_stream_mm(
                p, src=lambda g: Win_t[:, (g * 4) * KD * 128:(g + 1) * 4 * KD * 128],
                bufs=win_b, Mt=32, Kt=KD, N=CH,
                rhs=lambda ki: xn_bf[:, ki * CH:(ki + 1) * CH],
                evac=win_evac, ps=ps, group_mt=4)

            def sz_store(eng, P, ch=ch):
                for ti in range(KI):
                    _dma(eng, P, "dmg",
                         siluz_o[ti * 128:(ti + 1) * 128, ch * CH:(ch + 1) * CH],
                         A_sz[:, ti * CH:(ti + 1) * CH])
            p.seg("gpsimd", provides={"dmg": 16 * KI})(sz_store)

            def conv_v(eng, P):
                for ti in range(KI):
                    base = ti * (CH + 3)
                    bsl = slice(ti * CH, (ti + 1) * CH)
                    eng.tensor_scalar_mul(
                        out=B_acc[:, bsl], in0=xin[:, base:base + CH],
                        scalar1=convw[:, ti * 4:ti * 4 + 1])
                    eng.drain()
                    eng.scalar_tensor_tensor(
                        out=C_dp[:, bsl], in0=xin[:, base + 1:base + 1 + CH],
                        scalar=convw[:, ti * 4 + 1:ti * 4 + 2],
                        in1=B_acc[:, bsl], op0=ALU.mult, op1=ALU.add)
                    eng.drain()
                    eng.scalar_tensor_tensor(
                        out=B_acc[:, bsl], in0=xin[:, base + 2:base + 2 + CH],
                        scalar=convw[:, ti * 4 + 2:ti * 4 + 3],
                        in1=C_dp[:, bsl], op0=ALU.mult, op1=ALU.add)
                    eng.drain()
                    ins = eng.scalar_tensor_tensor(
                        out=C_dp[:, bsl], in0=xin[:, base + 3:base + 3 + CH],
                        scalar=convw[:, ti * 4 + 3:ti * 4 + 4],
                        in1=B_acc[:, bsl], op0=ALU.mult, op1=ALU.add)
                    eng.drain()
                ins.then_inc(P.sems["tick"], 1)
            p.seg("vector", provides={"tick": 1})(conv_v)

            def conv_s(eng, P):
                for ti in range(KI):
                    ins = eng.activation(
                        out=B_bf[:, ti * CH:(ti + 1) * CH],
                        in_=C_dp[:, ti * CH:(ti + 1) * CH], func=AF.Silu,
                        bias=vecs["convb"][:, ti:ti + 1])
                ins.then_inc(P.sems["tick"], 1)
            p.seg("scalar", provides={"tick": 1})(conv_s)

            if ch < NCH - 1:
                def carry(eng, P):
                    for ti in range(KI):
                        base = ti * (CH + 3)
                        ins = eng.tensor_copy(
                            out=xin[:, base:base + 3],
                            in_=xin[:, base + CH:base + CH + 3])
                    ins.then_inc(P.sems["tick"], 1)
                p.seg("vector", provides={"tick": 1})(carry)

            def xc_store(eng, P, ch=ch):
                for ti in range(KI):
                    _dma(eng, P, "dms",
                         xconv_o[ti * 128:(ti + 1) * 128, ch * CH:(ch + 1) * CH],
                         B_bf[:, ti * CH:(ti + 1) * CH])
            p.seg("sync", provides={"dms": 16 * KI})(xc_store)

            def xp_t(eng, P):
                for ki in range(KI):
                    ins = eng.matmul(ps_s2[0:32, 0:CH],
                                     wxp[:, ki * 32:(ki + 1) * 32],
                                     B_bf[:, ki * CH:(ki + 1) * CH],
                                     start=(ki == 0), stop=(ki == KI - 1))
                ins.then_inc(P.sems["mm"], 1)
            p.seg("tensor", provides={"mm": 1})(xp_t)

            def xp_ev(eng, P):
                eng.activation(out=tmp_t[0:32, 0:CH], in_=ps_s2[0:32, 0:CH],
                               func=AF.Identity,
                               bias=bxp[0:32, 0:1]).then_inc(P.sems["ev"], 1)
            p.seg("scalar", provides={"ev": 1})(xp_ev)

            def bc_store(eng, P, ch=ch):
                _dma(eng, P, "dms", bc_o[:, ch * CH:(ch + 1) * CH],
                     tmp_t[0:32, 0:CH])
            p.seg("sync", provides={"dms": 16})(bc_store)

            emit_stream_mm(
                p, src=lambda g: Wdt_t[:, (g * 2) * KI * 128:(g + 1) * 2 * KI * 128],
                bufs=wdt_b, Mt=KI, Kt=KI, N=CH,
                rhs=lambda ki: B_bf[:, ki * CH:(ki + 1) * CH],
                evac=lambda mi, eng, bank: eng.activation(
                    out=C_dp[:, mi * CH:(mi + 1) * CH], in_=bank[:, 0:CH],
                    func=AF.Identity, bias=vecs["bdt"][:, mi:mi + 1]),
                ps=ps, group_mt=2)

            def sp_s(eng, P):
                for ti in range(KI):
                    sl = slice(ti * CH, (ti + 1) * CH)
                    eng.activation(out=A_sz[:, sl], in_=C_dp[:, sl], func=AF.Abs)
                    eng.drain()
                    eng.activation(out=A_sz[:, sl], in_=A_sz[:, sl], func=AF.Exp,
                                   scale=-1.0)
                    eng.drain()
                    ins = eng.activation(out=A_sz[:, sl], in_=A_sz[:, sl],
                                         func=AF.Ln, bias=1.0)
                    eng.drain()
                ins.then_inc(P.sems["tick"], 1)
            p.seg("scalar", provides={"tick": 1})(sp_s)

            def sp_v(eng, P):
                for ti in range(KI):
                    sl = slice(ti * CH, (ti + 1) * CH)
                    eng.tensor_scalar_max(out=B_acc[:, sl], in0=C_dp[:, sl],
                                          scalar1=0.0)
                    eng.drain()
                    ins = eng.tensor_add(out=C_dp[:, sl], in0=A_sz[:, sl],
                                         in1=B_acc[:, sl])
                    eng.drain()
                ins.then_inc(P.sems["tick"], 1)
            p.seg("vector", provides={"tick": 1})(sp_v)

            def d_store(eng, P, ch=ch):
                for ti in range(KI):
                    _dma(eng, P, "dms",
                         delta_o[ti * 128:(ti + 1) * 128, ch * CH:(ch + 1) * CH],
                         C_dp[:, ti * CH:(ti + 1) * CH])
            p.seg("sync", provides={"dms": 16 * KI})(d_store)

        p.build(ctx)
    return nc


def _w_image(W):
    K, M = W.shape
    a = W.reshape(K // 128, 128, M // 128, 128)
    return np.ascontiguousarray(
        a.transpose(1, 2, 0, 3).reshape(128, (K // 128) * (M // 128) * 128))


def _pack_v(v):
    n = v.shape[0]
    return np.ascontiguousarray(v.reshape(n // 128, 128).T)


def _l4_w_images(inputs):
    return np.concatenate([
        _w_image(np.ascontiguousarray(inputs["L1_attn_fc1_w"].T)),
        _w_image(np.ascontiguousarray(inputs["L1_attn_fc2_w"].T)),
        _w_image(np.ascontiguousarray(inputs["L1_ffn_w1"].T)),
        _w_image(np.ascontiguousarray(inputs["L1_ffn_w2"].T))], axis=1)


def _l1_inmaps(inputs, x_cm, l4img=None):
    Win = _w_image(inputs["L0_attn_W_in"])
    Wdt = _w_image(inputs["L0_attn_W_dt"])
    Wxp = np.ascontiguousarray(
        inputs["L0_attn_W_xp"].reshape(KI, 128, 32).transpose(1, 0, 2)
        .reshape(128, KI * 32))
    b_in = inputs["L0_attn_b_in"]
    conv_w = inputs["L0_attn_conv_w"][:, 0, :]
    shared = {
        "Win_t": Win, "Wdt_t": Wdt, "Wxp_t": Wxp,
        "b_in_x": _pack_v(b_in[:DIN]),
        "b_in_z": _pack_v(b_in[DIN:]),
        "conv_w_p": np.ascontiguousarray(
            conv_w.reshape(KI, 128, 4).transpose(1, 0, 2).reshape(128, KI * 4)),
        "conv_b_p": _pack_v(inputs["L0_attn_conv_b"]),
        "b_dt_p": _pack_v(inputs["L0_attn_b_dt"]),
        "b_xp_p": np.ascontiguousarray(
            np.pad(inputs["L0_attn_b_xp"], (0, 96)).reshape(1, 128).T),
    }
    if l4img is None:
        l4img = _l4_w_images(inputs)
    maps = []
    for c in range(NCORE):
        b, h = c // 2, c % 2
        t0 = h * TPC
        halo = np.zeros((D, 4), np.float32)
        if h == 1:
            halo[:, 0:3] = x_cm[b][:, t0 - 3:t0]
        m = dict(shared)
        m.update({
            "x_part": np.ascontiguousarray(x_cm[b][:, t0:t0 + TPC]),
            "x_halo": halo,
            "halo_mask": np.full((128, 1), float(h), np.float32),
            "l4wf": np.ascontiguousarray(l4img[:, c * 16384:(c + 1) * 16384]),
        })
        maps.append(m)
    return maps


_L1_NC = None


def run_l1(inputs, x_cm):
    global _L1_NC
    if _L1_NC is None:
        _L1_NC = build_l1()
    res = run_bass_kernel_spmd(_L1_NC, _l1_inmaps(inputs, x_cm),
                               core_ids=list(range(NCORE)))
    return res.results




TCH = 512
NTCH = S // TCH
DTL = 8


def build_l2():
    nc = bass.Bass("TRN2", target_bir_lowering=False)
    delta_p = nc.dram_tensor("delta_p", [1024, S], F32, kind="ExternalInput").ap()
    xconv_p = nc.dram_tensor("xconv_p", [1024, S], BF16, kind="ExternalInput").ap()
    bc_p = nc.dram_tensor("bc_p", [32, S], F32, kind="ExternalInput").ap()
    A_p = nc.dram_tensor("A_p", [128, DTL * DST], F32, kind="ExternalInput").ap()
    dsk_p = nc.dram_tensor("dsk_p", [128, DTL], F32, kind="ExternalInput").ap()
    y_o = nc.dram_tensor("y_o", [1024, S], F32, kind="ExternalOutput").ap()

    with ExitStack() as ctx:
        E = ctx.enter_context
        p = Prog(nc, ctx)
        sb = lambda nm, sh, dt=F32: E(nc.sbuf_tensor(nm, sh, dt))

        d_t = [sb(f"d_t{i}", [128, TCH]) for i in range(2)]
        xc_t = [sb(f"xc_t{i}", [128, TCH], BF16) for i in range(2)]
        dX = sb("dX", [128, TCH], BF16)
        dA_all = [sb(f"dA_all{i}", [128, DST * TCH], BF16) for i in range(2)]
        dbx_all = sb("dbx_all", [128, DST * TCH], BF16)
        hbuf = [sb(f"hbuf{i}", [128, DST * TCH], BF16) for i in range(2)]
        ybuf = sb("ybuf", [128, DST * TCH], BF16)
        yred = [sb(f"yred{i}", [128, TCH]) for i in range(2)]
        B_rep = sb("B_rep", [128, DST * TCH], BF16)
        C_rep = sb("C_rep", [128, DST * TCH], BF16)
        bc_row = sb("bc_row", [1, 32 * TCH])
        carry = sb("carry", [128, DTL * DST])
        yf = sb("yf", [128, TCH])
        A_sb = sb("A_sb", [128, DTL * DST])
        A_n = sb("A_n", [128, DTL * DST])
        dsk = sb("dsk", [128, DTL])
        ones_row = sb("ones_rowL2", [1, 128])

        ps_r = [E(nc.psum_tensor(f"psr{i}", [128, TCH], F32)) for i in range(2)]

        def init_ld(eng, P):
            _dma(eng, P, "dms", A_sb[:], A_p)
            _dma(eng, P, "dms", dsk[:], dsk_p)
        p.seg("sync", provides={"dms": 32})(init_ld)

        def init_v(eng, P):
            eng.memset(ones_row[:], 1.0)
            eng.memset(carry[:], 0.0).then_inc(P.sems["tick"], 1)
        p.seg("vector", provides={"tick": 1})(init_v)

        def init_s(eng, P):
            eng.activation(out=A_n[:], in_=A_sb[:],
                           func=AF.Exp).then_inc(P.sems["tick"], 1)
        p.seg("scalar", provides={"tick": 1})(init_s)

        def init_v2(eng, P):
            eng.tensor_scalar_mul(out=A_sb[:], in0=A_n[:],
                                  scalar1=-1.0).then_inc(P.sems["tick"], 1)
        p.seg("vector", provides={"tick": 1})(init_v2)

        pend_v3 = [None]

        for tch in range(NTCH):
            tsl = slice(tch * TCH, (tch + 1) * TCH)

            def bc_ld(eng, P, tch=tch):
                _dma(eng, P, "dms", bc_row[:],
                     bc_p[:, tch * TCH:(tch + 1) * TCH])
            p.seg("sync", provides={"dms": 16})(bc_ld)

            mm0, ev0 = p.base("mm"), p.base("ev")
            w0 = dict(p.tot)

            def rep_t(eng, P, mm0=mm0, ev0=ev0):
                for s in range(32):
                    if s >= 2:
                        eng.wait_ge(P.sems["ev"], ev0 + s - 1)
                    eng.matmul(ps_r[s % 2][:], ones_row[0:1, :],
                               bc_row[0:1, s * TCH:(s + 1) * TCH],
                               start=True, stop=True).then_inc(P.sems["mm"], 1)
            p.seg("tensor", provides={"mm": 32}, waits=w0)(rep_t)

            def rep_s(eng, P, mm0=mm0):
                for s in range(32):
                    eng.wait_ge(P.sems["mm"], mm0 + s + 1)
                    if s < DST:
                        out = B_rep[:, s * TCH:(s + 1) * TCH]
                    else:
                        out = C_rep[:, (s - DST) * TCH:(s - DST + 1) * TCH]
                    eng.copy(out=out, in_=ps_r[s % 2][:]).then_inc(P.sems["ev"], 1)
            p.seg("scalar", provides={"ev": 32}, waits=w0)(rep_s)

            for dt_i in range(DTL):
                buf = dt_i % 2
                if tch == 0 and dt_i == 0:
                    def ld0(eng, P):
                        _dma(eng, P, "dwa", d_t[0][:], delta_p[0:128, 0:TCH])
                    p.seg("sync", provides={"dwa": 16})(ld0)

                    def ld0g(eng, P):
                        _dma(eng, P, "dxa", xc_t[0][:], xconv_p[0:128, 0:TCH])
                    p.seg("gpsimd", provides={"dxa": 16})(ld0g)

                w1 = dict(p.tot)

                def v1(eng, P, buf=buf):
                    eng.tensor_mul(out=dX[:], in0=d_t[buf][:],
                                   in1=xc_t[buf][:])
                    eng.drain()
                    for s in range(DST):
                        ins = eng.tensor_mul(
                            out=dbx_all[:, s * TCH:(s + 1) * TCH],
                            in0=dX[:], in1=B_rep[:, s * TCH:(s + 1) * TCH])
                    ins.then_inc(P.sems["tick"], 1)
                p.seg("vector", provides={"tick": 1}, waits=w1)(v1)

                def s1(eng, P, buf=buf, dt_i=dt_i):
                    for s in range(DST):
                        ins = eng.activation(
                            out=dA_all[buf][:, s * TCH:(s + 1) * TCH],
                            in_=d_t[buf][:], func=AF.Exp,
                            scale=A_sb[:, dt_i * DST + s:dt_i * DST + s + 1])
                    ins.then_inc(P.sems["tick"], 1)
                p.seg("scalar", provides={"tick": 1}, waits=w1)(s1)

                def v2(eng, P, dt_i=dt_i, buf=buf):
                    for s in range(DST):
                        eng.tensor_tensor_scan(
                            out=hbuf[buf][:, s * TCH:(s + 1) * TCH],
                            data0=dA_all[buf][:, s * TCH:(s + 1) * TCH],
                            data1=dbx_all[:, s * TCH:(s + 1) * TCH],
                            initial=carry[:, dt_i * DST + s:dt_i * DST + s + 1],
                            op0=ALU.mult, op1=ALU.add)
                    eng.drain()
                    ins = eng.tensor_copy(
                        out=carry[:, dt_i * DST:(dt_i + 1) * DST].rearrange(
                            "p (s o) -> p s o", o=1),
                        in_=hbuf[buf][:].rearrange("p (s t) -> p s t", t=TCH)[
                            :, :, TCH - 1:TCH])
                    eng.drain()
                    ins.then_inc(P.sems["tick"], 1)
                p.seg("vector", provides={"tick": 1})(v2)

                def g2(eng, P, buf=buf):
                    eng.tensor_mul(out=ybuf[:], in0=hbuf[buf][:],
                                   in1=C_rep[:]).then_inc(P.sems["tick"], 1)
                p.seg("gpsimd", provides={"tick": 1})(g2)

                w3 = dict(p.tot)

                def v3(eng, P, dt_i=dt_i, buf=buf):
                    T2 = TCH
                    scr = dA_all[buf]
                    for s in range(8):
                        eng.tensor_add(out=scr[:, s * T2:(s + 1) * T2],
                                       in0=ybuf[:, (2 * s) * T2:(2 * s + 1) * T2],
                                       in1=ybuf[:, (2 * s + 1) * T2:(2 * s + 2) * T2])
                    eng.drain()
                    for s in range(4):
                        eng.tensor_add(out=scr[:, (8 + s) * T2:(9 + s) * T2],
                                       in0=scr[:, (2 * s) * T2:(2 * s + 1) * T2],
                                       in1=scr[:, (2 * s + 1) * T2:(2 * s + 2) * T2])
                    eng.drain()
                    for s in range(2):
                        eng.tensor_add(out=scr[:, s * T2:(s + 1) * T2],
                                       in0=scr[:, (8 + 2 * s) * T2:(9 + 2 * s) * T2],
                                       in1=scr[:, (9 + 2 * s) * T2:(10 + 2 * s) * T2])
                    eng.drain()
                    eng.tensor_add(out=scr[:, 2 * T2:3 * T2],
                                   in0=scr[:, 0:T2], in1=scr[:, T2:2 * T2])
                    eng.drain()
                    ins = eng.scalar_tensor_tensor(
                        out=yred[buf][:], in0=xc_t[buf][:],
                        scalar=dsk[:, dt_i:dt_i + 1],
                        in1=scr[:, 2 * T2:3 * T2],
                        op0=ALU.mult, op1=ALU.add)
                    eng.drain()
                    ins.then_inc(P.sems["tick"], 1)

                def st(eng, P, dt_i=dt_i, tch=tch, buf=buf):
                    _dma(eng, P, "dms",
                         y_o[dt_i * 128:(dt_i + 1) * 128,
                             tch * TCH:(tch + 1) * TCH], yred[buf][:])
                p.seg("vector", provides={"tick": 1})(v3)
                p.seg("sync", provides={"dms": 16})(st)

                nxt = (tch * DTL + dt_i + 1)
                if nxt < NTCH * DTL:
                    ntch, ndt = nxt // DTL, nxt % DTL

                    def ldn(eng, P, ntch=ntch, ndt=ndt, nb=(nxt % 2)):
                        _dma(eng, P, ("dwa", "dwb")[nb], d_t[nb][:],
                             delta_p[ndt * 128:(ndt + 1) * 128,
                                     ntch * TCH:(ntch + 1) * TCH])
                    p.seg("sync", provides={("dwa", "dwb")[nxt % 2]: 16})(ldn)

                    def ldng(eng, P, ntch=ntch, ndt=ndt, nb=(nxt % 2)):
                        _dma(eng, P, ("dxa", "dxb")[nb], xc_t[nb][:],
                             xconv_p[ndt * 128:(ndt + 1) * 128,
                                     ntch * TCH:(ntch + 1) * TCH])
                    p.seg("gpsimd", provides={("dxa", "dxb")[nxt % 2]: 16})(ldng)

        p.build(ctx)
    return nc


def _l2_inmaps(inputs, DELTA, XCONV, BCF):
    A_log = inputs["L0_attn_A_log"]
    D_skip = inputs["L0_attn_D_skip"]
    maps = []
    for c in range(NCORE):
        b, dh = c // 2, c % 2
        rsl = slice(dh * 1024, (dh + 1) * 1024)
        A_sl = A_log[rsl].reshape(DTL, 128, DST).transpose(1, 0, 2).reshape(
            128, DTL * DST)
        dsk_sl = D_skip[rsl].reshape(DTL, 128).T
        maps.append({
            "delta_p": np.ascontiguousarray(DELTA[b][rsl]),
            "xconv_p": np.ascontiguousarray(XCONV[b][rsl]),
            "bc_p": np.ascontiguousarray(BCF[b]),
            "A_p": np.ascontiguousarray(A_sl),
            "dsk_p": np.ascontiguousarray(dsk_sl),
        })
    return maps


_L2_NC = None


def run_l2(inputs, DELTA, XCONV, BCF):
    global _L2_NC
    if _L2_NC is None:
        _L2_NC = build_l2()
    res = run_bass_kernel_spmd(_L2_NC, _l2_inmaps(inputs, DELTA, XCONV, BCF),
                               core_ids=list(range(NCORE)))
    return res.results




def build_l3():
    nc = bass.Bass("TRN2", target_bir_lowering=False)
    x_part = nc.dram_tensor("x_part", [D, TPC], F32, kind="ExternalInput").ap()
    y_part = nc.dram_tensor("y_part", [DIN, TPC], F32, kind="ExternalInput").ap()
    sz_part = nc.dram_tensor("sz_part", [DIN, TPC], BF16, kind="ExternalInput").ap()
    Wout_t = nc.dram_tensor("Wout_t", [128, 8 * KI * 128], F32, kind="ExternalInput").ap()
    WT_t = nc.dram_tensor("WT_t", [128, KD * KD * 128], F32, kind="ExternalInput").ap()
    M0_t = nc.dram_tensor("M0_t", [128, KD * 64], F32, kind="ExternalInput").ap()
    dsh_t = nc.dram_tensor("dsh_t", [64, KD * 128], F32, kind="ExternalInput").ap()
    b_out_p = nc.dram_tensor("b_out_p", [128, KD], F32, kind="ExternalInput").ap()
    alpha_p = nc.dram_tensor("alpha_p", [128, KD], F32, kind="ExternalInput").ap()
    beta_p = nc.dram_tensor("beta_p", [128, KD], F32, kind="ExternalInput").ap()
    x1_o = nc.dram_tensor("x1_o", [D, TPC], F32, kind="ExternalOutput").ap()

    with ExitStack() as ctx:
        E = ctx.enter_context
        p = Prog(nc, ctx)
        sb = lambda nm, sh, dt=F32: E(nc.sbuf_tensor(nm, sh, dt))

        wout = sb("wout", [128, 8 * KI * 128], BF16)
        wt = sb("wt", [128, KD * KD * 128], BF16)
        m0 = sb("m0", [128, KD * 64], BF16)
        dsh = sb("dsh", [64, KD * 128], BF16)
        x_f = sb("x_f", [128, KD * CH])
        x_bf = sb("x_bf", [128, KD * CH], BF16)
        xsq = sb("xsq", [128, KD * CH], BF16)
        xn_bf = sb("xn_bf", [128, KD * CH], BF16)
        ybuf = sb("ybuf", [128, KI * CH])
        szb = sb("szb", [128, KI * CH], BF16)
        y2_bf = sb("y2_bf", [128, KI * CH], BF16)
        r_bf = sb("r_bf", [128, KD * CH], BF16)
        f_out = sb("f_out", [128, KD * CH])
        pool_bf = sb("pool_bf", [64, CH], BF16)
        vb = {nm: sb("v_" + nm, [128, KD]) for nm in ("bout", "alpha", "beta")}
        ones_col = sb("ones_colL3", [128, 1], BF16)
        ones_row = sb("ones_rowL3", [1, 128], F32)
        cst = sb("cstL3", [1, 4], F32)
        rows = {nm: sb("rowL3_" + nm, [1, CH]) for nm in
                ("mu", "ex2", "var", "sd", "r", "mur")}
        tmp_t = sb("tmpL3", [128, CH], F32)
        tmp2_t = sb("tmp2L3", [128, CH], F32)

        ps = [E(nc.psum_tensor(f"psL3{i}", [128, CH], F32)) for i in range(4)]
        ps_s1 = E(nc.psum_tensor("psL3_s1", [128, CH], F32))
        ps_s2 = E(nc.psum_tensor("psL3_s2", [128, CH], F32))
        ps_rR = E(nc.psum_tensor("psL3_rR", [128, CH], F32))
        ps_rM = E(nc.psum_tensor("psL3_rM", [128, CH], F32))

        def init_ld(eng, P):
            for nm, srcv in (("bout", b_out_p), ("alpha", alpha_p),
                             ("beta", beta_p)):
                _dma(eng, P, "dms", vb[nm][:], srcv)
        p.seg("sync", provides={"dms": 48})(init_ld)

        def init_w(eng, P):
            _dma(eng, P, "dmw", wout[:], Wout_t)
            _dma(eng, P, "dmw", wt[:], WT_t)
            _dma(eng, P, "dmw", m0[:], M0_t)
            _dma(eng, P, "dmw", dsh[:], dsh_t)
        p.seg("gpsimd", provides={"dmw": 64})(init_w)

        def init_v(eng, P):
            eng.memset(ones_col[:], 1.0)
            eng.memset(ones_row[:], 1.0)
            eng.memset(cst[0:1, 0:1], 1e-5).then_inc(P.sems["tick"], 1)
        p.seg("vector", provides={"tick": 1})(init_v)

        for ch in range(NCH):
            csl = slice(ch * CH, (ch + 1) * CH)

            def ld_f(eng, P, ch=ch):
                for ki in range(KD):
                    _dma(eng, P, "dms", x_f[:, ki * CH:(ki + 1) * CH],
                         x_part[ki * 128:(ki + 1) * 128, ch * CH:(ch + 1) * CH])
                for ti in range(KI):
                    _dma(eng, P, "dms", ybuf[:, ti * CH:(ti + 1) * CH],
                         y_part[ti * 128:(ti + 1) * 128, ch * CH:(ch + 1) * CH])
            p.seg("sync", provides={"dms": 16 * (KD + KI)})(ld_f)

            def ld_g(eng, P, ch=ch):
                for ki in range(KD):
                    _dma(eng, P, "dmg", x_bf[:, ki * CH:(ki + 1) * CH],
                         x_part[ki * 128:(ki + 1) * 128, ch * CH:(ch + 1) * CH])
                for ti in range(KI):
                    _dma(eng, P, "dmg", szb[:, ti * CH:(ti + 1) * CH],
                         sz_part[ti * 128:(ti + 1) * 128, ch * CH:(ch + 1) * CH])
            p.seg("gpsimd", provides={"dmg": 16 * (KD + KI)})(ld_g)

            def v_y2(eng, P):
                for ti in range(KI):
                    sl = slice(ti * CH, (ti + 1) * CH)
                    ins = eng.tensor_mul(out=y2_bf[:, sl], in0=ybuf[:, sl],
                                         in1=szb[:, sl])
                ins.then_inc(P.sems["tick"], 1)
            p.seg("vector", provides={"tick": 1})(v_y2)

            emit_ln(p, KD, CH,
                    xbf=lambda ki: x_bf[:, ki * CH:(ki + 1) * CH],
                    sq_buf=lambda ki: xsq[:, ki * CH:(ki + 1) * CH],
                    rows=rows, ps_s1=ps_s1, ps_s2=ps_s2, ps_rR=ps_rR, ps_rM=ps_rM,
                    write=lambda ki, eng, t, psM: eng.tensor_sub(
                        out=xn_bf[:, ki * CH:(ki + 1) * CH], in0=t, in1=psM),
                    tmp=lambda ki: tmp_t[:, 0:CH],
                    ones_col=ones_col, ones_row=ones_row, mode="plain",
                    eps_ap=cst[0:1, 0:1])

            emit_res_mm(
                p, wsb=lambda ki, mi: wout[:, (mi * KI + ki) * 128:
                                           (mi * KI + ki + 1) * 128],
                Mt=KD, Kt=KI, N=CH,
                rhs=lambda ki: y2_bf[:, ki * CH:(ki + 1) * CH],
                evac=lambda mi, eng, bank: eng.activation(
                    out=ybuf[:, mi * CH:(mi + 1) * CH], in_=bank[:, 0:CH],
                    func=AF.Identity, bias=vb["bout"][:, mi:mi + 1]),
                ps=ps)

            def v_r(eng, P):
                for ki in range(KD):
                    sl = slice(ki * CH, (ki + 1) * CH)
                    ins = eng.tensor_add(out=r_bf[:, sl], in0=xn_bf[:, sl],
                                         in1=ybuf[:, sl])
                ins.then_inc(P.sems["tick"], 1)
            p.seg("vector", provides={"tick": 1})(v_r)

            def wr_fin(ki, eng, t, psM):
                eng.tensor_sub(out=tmp2_t[:], in0=t, in1=psM)
                eng.drain()
                ins = eng.tensor_scalar(
                    out=ybuf[:, (KD + ki) * CH:(KD + ki + 1) * CH],
                    in0=tmp2_t[:], scalar1=vb["alpha"][:, ki:ki + 1],
                    scalar2=vb["beta"][:, ki:ki + 1],
                    op0=ALU.mult, op1=ALU.add)
                eng.drain()
                return ins
            emit_ln(p, KD, CH,
                    xbf=lambda ki: r_bf[:, ki * CH:(ki + 1) * CH],
                    sq_buf=lambda ki: xsq[:, ki * CH:(ki + 1) * CH],
                    rows=rows, ps_s1=ps_s1, ps_s2=ps_s2, ps_rR=ps_rR, ps_rM=ps_rM,
                    write=wr_fin, tmp=lambda ki: tmp_t[:, 0:CH],
                    ones_col=ones_col, ones_row=ones_row, mode="l0final",
                    eps=1e-6)

            def pool_t(eng, P):
                for ki in range(KD):
                    ins = eng.matmul(ps_s1[0:64, 0:CH],
                                     m0[:, ki * 64:(ki + 1) * 64],
                                     xn_bf[:, ki * CH:(ki + 1) * CH],
                                     start=(ki == 0), stop=(ki == KD - 1))
                ins.then_inc(P.sems["mm"], 1)
            p.seg("tensor", provides={"mm": 1})(pool_t)

            def pool_ev(eng, P):
                eng.copy(out=pool_bf[:], in_=ps_s1[0:64, 0:CH]).then_inc(
                    P.sems["ev"], 1)
            p.seg("scalar", provides={"ev": 1})(pool_ev)

            def f_extra(mi, eng, bank):
                return eng.matmul(bank[:, 0:CH],
                                  dsh[:, mi * 128:(mi + 1) * 128],
                                  pool_bf[:], start=False, stop=True)
            emit_res_mm(
                p, wsb=lambda ki, mi: wt[:, (mi * KD + ki) * 128:
                                         (mi * KD + ki + 1) * 128],
                Mt=KD, Kt=KD, N=CH,
                rhs=lambda ki: xn_bf[:, ki * CH:(ki + 1) * CH],
                evac=lambda mi, eng, bank: eng.copy(
                    out=f_out[:, mi * CH:(mi + 1) * CH], in_=bank[:, 0:CH]),
                ps=ps, mm_extra=f_extra)

            def v_x1(eng, P):
                for ki in range(KD):
                    sl = slice(ki * CH, (ki + 1) * CH)
                    eng.tensor_add(out=tmp_t[:], in0=x_f[:, sl],
                                   in1=ybuf[:, (KD + ki) * CH:(KD + ki + 1) * CH])
                    eng.drain()
                    ins = eng.tensor_add(out=ybuf[:, sl], in0=tmp_t[:],
                                         in1=f_out[:, sl])
                    eng.drain()
                ins.then_inc(P.sems["tick"], 1)
            p.seg("vector", provides={"tick": 1})(v_x1)

            def st_x1(eng, P, ch=ch):
                for ki in range(KD):
                    _dma(eng, P, "dms",
                         x1_o[ki * 128:(ki + 1) * 128, ch * CH:(ch + 1) * CH],
                         ybuf[:, ki * CH:(ki + 1) * CH])
            p.seg("sync", provides={"dms": 16 * KD})(st_x1)

        p.build(ctx)
    return nc


def _l3_inmaps(inputs, x_cm, Y, SZ):
    Wout = _w_image(inputs["L0_attn_W_out"])
    WT = _w_image(np.ascontiguousarray(inputs["L0_ffn_weight"].T))
    M0 = (np.arange(1024)[:, None] % 64 == np.arange(64)[None, :]).astype(np.float32)
    M0i = np.ascontiguousarray(
        M0.reshape(KD, 128, 64).transpose(1, 0, 2).reshape(128, KD * 64))
    dsh = np.ascontiguousarray(inputs["L0_ffn_disha"])
    shared = {
        "Wout_t": Wout, "WT_t": WT, "M0_t": M0i, "dsh_t": dsh,
        "b_out_p": _pack_v(inputs["L0_attn_b_out"]),
        "alpha_p": _pack_v(inputs["L0_attn_alpha"]),
        "beta_p": _pack_v(inputs["L0_attn_beta"]),
    }
    maps = []
    for c in range(NCORE):
        b, h = c // 2, c % 2
        t0 = h * TPC
        m = dict(shared)
        m.update({
            "x_part": np.ascontiguousarray(x_cm[b][:, t0:t0 + TPC]),
            "y_part": np.ascontiguousarray(Y[b][:, t0:t0 + TPC]),
            "sz_part": np.ascontiguousarray(SZ[b][:, t0:t0 + TPC]),
        })
        maps.append(m)
    return maps


_L3_NC = None


def run_l3(inputs, x_cm, Y, SZ):
    global _L3_NC
    if _L3_NC is None:
        _L3_NC = build_l3()
    res = run_bass_kernel_spmd(_L3_NC, _l3_inmaps(inputs, x_cm, Y, SZ),
                               core_ids=list(range(NCORE)))
    return res.results




KH = 32


def build_l4():
    nc = bass.Bass("TRN2", target_bir_lowering=False)
    x1_p = nc.dram_tensor("x1_p", [D, TPC], F32, kind="ExternalInput").ap()
    projT_t = nc.dram_tensor("projT_t", [128, KD * KD * 128], F32, kind="ExternalInput").ap()
    fc1T_t = nc.dram_tensor("fc1T_t", [128, KH * KD * 128], BF16, kind="ExternalInput").ap()
    fc2T_t = nc.dram_tensor("fc2T_t", [128, KD * KH * 128], BF16, kind="ExternalInput").ap()
    w1T_t = nc.dram_tensor("w1T_t", [128, KH * KD * 128], BF16, kind="ExternalInput").ap()
    w2T_t = nc.dram_tensor("w2T_t", [128, KD * KH * 128], BF16, kind="ExternalInput").ap()
    wr_t = nc.dram_tensor("wr_t", [128, KD * 16], F32, kind="ExternalInput").ap()
    oh_t = nc.dram_tensor("oh_t", [16, 16 * 128], F32, kind="ExternalInput").ap()
    id_t = nc.dram_tensor("id_t", [128, 128], F32, kind="ExternalInput").ap()
    vec_names = ("proj_b", "ln1_g", "ln1_b", "ln2_g", "ln2_b", "fc2_b",
                 "b2", "g1", "g2")
    vins = {nm: nc.dram_tensor("i_" + nm, [128, KD], F32,
                               kind="ExternalInput").ap() for nm in vec_names}
    vec32 = ("lnh_g", "lnh_b", "fc1_b", "b1")
    vins32 = {nm: nc.dram_tensor("i_" + nm, [128, KH], F32,
                                 kind="ExternalInput").ap() for nm in vec32}
    brt = nc.dram_tensor("i_brt", [128, 1], F32, kind="ExternalInput").ap()
    x2_o = nc.dram_tensor("x2_o", [D, TPC], F32, kind="ExternalOutput").ap()

    with ExitStack() as ctx:
        E = ctx.enter_context
        p = Prog(nc, ctx)
        sb = lambda nm, sh, dt=F32: E(nc.sbuf_tensor(nm, sh, dt))

        projT = sb("projT", [128, KD * KD * 128], BF16)
        wr = sb("wr", [128, KD * 16], BF16)
        oh = sb("oh", [16, 16 * 128])
        idm = sb("idm", [128, 128])
        wb = [sb(f"wbL4{i}", [128, 4 * KD * 128], BF16) for i in range(2)]
        x1f = sb("x1f", [128, KD * CH])
        x1bf = sb("x1bf", [128, KD * CH], BF16)
        xsq = sb("xsqL4", [128, KD * CH], BF16)
        xn1 = sb("xn1", [128, KD * CH], BF16)
        y1z2 = sb("y1z2", [128, KD * CH], BF16)
        p_bf = sb("p_bfL4", [128, KD * CH], BF16)
        t_bf = sb("t_bfL4", [128, KD * CH], BF16)
        hbb = sb("hbb", [128, KH * CH], BF16)
        sqp = sb("sqp", [128, 2 * CH], BF16)
        m_bf = sb("m_bfL4", [128, KD * CH], BF16)
        o_f = sb("o_fL4", [128, KD * CH])
        lgt = sb("lgt", [16, CH])
        lt = sb("lt", [128, DST])
        lta = sb("lta", [128, DST])
        ltb = sb("ltb", [128, DST])
        thrc = sb("thrc", [128, 4])
        thr_row = sb("thr_row", [1, CH])
        mask_cm = sb("mask_cm", [16, CH])
        mrow = sb("mrow", [128, 1])
        vb = {nm: sb("vL4_" + nm, [128, KD]) for nm in vec_names}
        vb32 = {nm: sb("vL4_" + nm, [128, KH]) for nm in vec32}
        brt_s = sb("brt_s", [128, 1])
        ones_col = sb("ones_colL4", [128, 1], BF16)
        ones_row = sb("ones_rowL4", [1, 128], F32)
        cst = sb("cstL4", [1, 4], F32)
        rows = {nm: sb("rowL4_" + nm, [1, CH]) for nm in
                ("mu", "ex2", "var", "sd", "r", "mur")}
        tmp_t = sb("tmpL4", [128, CH], F32)
        tmp2_t = sb("tmp2L4", [128, CH], F32)

        ps = [E(nc.psum_tensor(f"psL4{i}", [128, CH], F32)) for i in range(4)]
        ps_s1 = E(nc.psum_tensor("psL4_s1", [128, CH], F32))
        ps_s2 = E(nc.psum_tensor("psL4_s2", [128, CH], F32))
        ps_rR = E(nc.psum_tensor("psL4_rR", [128, CH], F32))
        ps_rM = E(nc.psum_tensor("psL4_rM", [128, CH], F32))

        def hm(ti):
            srcs = [t_bf, y1z2, x1bf, xsq]
            return srcs[ti // KD][:, (ti % KD) * CH:(ti % KD + 1) * CH]

        def init_ld(eng, P):
            for nm in vec_names:
                _dma(eng, P, "dms", vb[nm][:], vins[nm])
            for nm in vec32:
                _dma(eng, P, "dms", vb32[nm][:], vins32[nm])
            _dma(eng, P, "dms", brt_s[:], brt)
            _dma(eng, P, "dms", oh[:], oh_t)
            _dma(eng, P, "dms", idm[:], id_t)
        p.seg("sync", provides={"dms": 16 * (9 + 4 + 3)})(init_ld)

        def init_w(eng, P):
            _dma(eng, P, "dmw", projT[:], projT_t)
            _dma(eng, P, "dmw", wr[:], wr_t)
        p.seg("gpsimd", provides={"dmw": 32})(init_w)

        def init_v(eng, P):
            eng.memset(ones_col[:], 1.0)
            eng.memset(ones_row[:], 1.0)
            eng.memset(cst[0:1, 0:1], 1e-5).then_inc(P.sems["tick"], 1)
        p.seg("vector", provides={"tick": 1})(init_v)

        def aff_write(dst, g, b_):
            def wr(ki, eng, t, psM):
                eng.tensor_sub(out=tmp2_t[:], in0=t, in1=psM)
                eng.drain()
                ins = eng.tensor_scalar(
                    out=dst(ki), in0=tmp2_t[:], scalar1=g[:, ki:ki + 1],
                    scalar2=b_[:, ki:ki + 1], op0=ALU.mult, op1=ALU.add)
                eng.drain()
                return ins
            return wr

        for ch in range(NCH):
            def ld_f(eng, P, ch=ch):
                for ki in range(KD):
                    _dma(eng, P, "dms", x1f[:, ki * CH:(ki + 1) * CH],
                         x1_p[ki * 128:(ki + 1) * 128, ch * CH:(ch + 1) * CH])
            p.seg("sync", provides={"dms": 16 * KD})(ld_f)

            def ld_g(eng, P, ch=ch):
                for ki in range(KD):
                    _dma(eng, P, "dmg", x1bf[:, ki * CH:(ki + 1) * CH],
                         x1_p[ki * 128:(ki + 1) * 128, ch * CH:(ch + 1) * CH])
            p.seg("gpsimd", provides={"dmg": 16 * KD})(ld_g)

            emit_ln(p, KD, CH,
                    xbf=lambda ki: x1bf[:, ki * CH:(ki + 1) * CH],
                    sq_buf=lambda ki: xsq[:, ki * CH:(ki + 1) * CH],
                    rows=rows, ps_s1=ps_s1, ps_s2=ps_s2, ps_rR=ps_rR, ps_rM=ps_rM,
                    write=lambda ki, eng, t, psM: eng.tensor_sub(
                        out=xn1[:, ki * CH:(ki + 1) * CH], in0=t, in1=psM),
                    tmp=lambda ki: tmp_t[:, 0:CH],
                    ones_col=ones_col, ones_row=ones_row, mode="plain",
                    eps_ap=cst[0:1, 0:1])

            emit_ln(p, KD, CH,
                    xbf=lambda ki: xn1[:, ki * CH:(ki + 1) * CH],
                    sq_buf=lambda ki: xsq[:, ki * CH:(ki + 1) * CH],
                    rows=rows, ps_s1=ps_s1, ps_s2=ps_s2, ps_rR=ps_rR, ps_rM=ps_rM,
                    write=aff_write(
                        lambda ki: y1z2[:, ki * CH:(ki + 1) * CH],
                        vb["ln1_g"], vb["ln1_b"]),
                    tmp=lambda ki: tmp_t[:, 0:CH],
                    ones_col=ones_col, ones_row=ones_row, mode="plain",
                    eps_ap=cst[0:1, 0:1])

            emit_res_mm(
                p, wsb=lambda ki, mi: projT[:, (mi * KD + ki) * 128:
                                            (mi * KD + ki + 1) * 128],
                Mt=KD, Kt=KD, N=CH,
                rhs=lambda ki: y1z2[:, ki * CH:(ki + 1) * CH],
                evac=lambda mi, eng, bank: eng.activation(
                    out=p_bf[:, mi * CH:(mi + 1) * CH], in_=bank[:, 0:CH],
                    func=AF.Identity, bias=vb["proj_b"][:, mi:mi + 1]),
                ps=ps)

            def v_t(eng, P):
                for ki in range(KD):
                    sl = slice(ki * CH, (ki + 1) * CH)
                    ins = eng.scalar_tensor_tensor(
                        out=t_bf[:, sl], in0=p_bf[:, sl],
                        scalar=vb["g1"][:, ki:ki + 1], in1=xn1[:, sl],
                        op0=ALU.mult, op1=ALU.add)
                ins.then_inc(P.sems["tick"], 1)
            p.seg("vector", provides={"tick": 1})(v_t)

            emit_ln(p, KD, CH,
                    xbf=lambda ki: t_bf[:, ki * CH:(ki + 1) * CH],
                    sq_buf=lambda ki: xsq[:, ki * CH:(ki + 1) * CH],
                    rows=rows, ps_s1=ps_s1, ps_s2=ps_s2, ps_rR=ps_rR, ps_rM=ps_rM,
                    write=aff_write(
                        lambda ki: y1z2[:, ki * CH:(ki + 1) * CH],
                        vb["ln2_g"], vb["ln2_b"]),
                    tmp=lambda ki: tmp_t[:, 0:CH],
                    ones_col=ones_col, ones_row=ones_row, mode="plain",
                    eps_ap=cst[0:1, 0:1])

            emit_stream_mm(
                p, src=lambda g: fc1T_t[:, (g * 4) * KD * 128:
                                        (g + 1) * 4 * KD * 128],
                bufs=wb, Mt=KH, Kt=KD, N=CH,
                rhs=lambda ki: y1z2[:, ki * CH:(ki + 1) * CH],
                evac=lambda mi, eng, bank: eng.activation(
                    out=hbb[:, mi * CH:(mi + 1) * CH], in_=bank[:, 0:CH],
                    func=AF.Gelu, bias=vb32["fc1_b"][:, mi:mi + 1]),
                ps=ps, group_mt=4)

            emit_ln(p, KH, CH,
                    xbf=lambda ki: hbb[:, ki * CH:(ki + 1) * CH],
                    sq_buf=lambda ki: sqp[:, (ki % 2) * CH:(ki % 2 + 1) * CH],
                    rows=rows, ps_s1=ps_s1, ps_s2=ps_s2, ps_rR=ps_rR, ps_rM=ps_rM,
                    write=aff_write(
                        lambda ki: hbb[:, ki * CH:(ki + 1) * CH],
                        vb32["lnh_g"], vb32["lnh_b"]),
                    tmp=lambda ki: tmp_t[:, 0:CH],
                    ones_col=ones_col, ones_row=ones_row, mode="plain",
                    eps_ap=cst[0:1, 0:1], sq_ping=True)

            emit_stream_mm(
                p, src=lambda g: fc2T_t[:, g * KH * 128:(g + 1) * KH * 128],
                bufs=wb, Mt=KD, Kt=KH, N=CH,
                rhs=lambda ki: hbb[:, ki * CH:(ki + 1) * CH],
                evac=lambda mi, eng, bank: eng.activation(
                    out=m_bf[:, mi * CH:(mi + 1) * CH], in_=bank[:, 0:CH],
                    func=AF.Identity, bias=vb["fc2_b"][:, mi:mi + 1]),
                ps=ps, group_mt=1)

            def rt_t(eng, P):
                for ki in range(KD):
                    ins = eng.matmul(ps_s1[0:16, 0:CH],
                                     wr[:, ki * 16:(ki + 1) * 16],
                                     xn1[:, ki * CH:(ki + 1) * CH],
                                     start=(ki == 0), stop=(ki == KD - 1))
                ins.then_inc(P.sems["mm"], 1)
            p.seg("tensor", provides={"mm": 1})(rt_t)

            def rt_ev(eng, P):
                eng.activation(out=lgt[:], in_=ps_s1[0:16, 0:CH],
                               func=AF.Identity,
                               bias=brt_s[0:16, 0:1]).then_inc(P.sems["ev"], 1)
            p.seg("scalar", provides={"ev": 1})(rt_ev)

            for blk in range(4):
                def tr1(eng, P, blk=blk):
                    eng.transpose(ps_rR[:, 0:16], lgt[:, blk * 128:(blk + 1) * 128],
                                  idm[0:16, 0:16]).then_inc(P.sems["mm"], 1)
                p.seg("tensor", provides={"mm": 1})(tr1)

                def tr1e(eng, P):
                    eng.copy(out=lt[:, 0:DST],
                             in_=ps_rR[:, 0:16]).then_inc(P.sems["ev"], 1)
                p.seg("scalar", provides={"ev": 1})(tr1e)

                def top4(eng, P, blk=blk):
                    eng.reduce_max(out=thrc[:, blk:blk + 1], in_=lt[:, 0:DST],
                                   axis=AX.X)
                    eng.drain()
                    cur = lt
                    for rnd, nxt in ((1, lta), (2, ltb), (3, lta)):
                        if rnd == 3:
                            nxt = lt
                        eng.tensor_scalar(out=tmp_t[:, 0:DST], in0=cur[:, 0:DST],
                                          scalar1=thrc[:, blk:blk + 1],
                                          scalar2=None, op0=ALU.is_ge)
                        eng.drain()
                        eng.scalar_tensor_tensor(
                            out=nxt[:, 0:DST], in0=tmp_t[:, 0:DST],
                            scalar=-1e30, in1=cur[:, 0:DST],
                            op0=ALU.mult, op1=ALU.add)
                        eng.drain()
                        ins = eng.reduce_max(out=thrc[:, blk:blk + 1],
                                             in_=nxt[:, 0:DST], axis=AX.X)
                        eng.drain()
                        cur = nxt
                    ins.then_inc(P.sems["tick"], 1)
                p.seg("vector", provides={"tick": 1})(top4)

                def tr2(eng, P, blk=blk):
                    eng.transpose(ps_rM[0:1, 0:128], thrc[:, blk:blk + 1],
                                  idm[:, :]).then_inc(P.sems["mm"], 1)
                p.seg("tensor", provides={"mm": 1})(tr2)

                def tr2e(eng, P, blk=blk):
                    eng.copy(out=thr_row[0:1, blk * 128:(blk + 1) * 128],
                             in_=ps_rM[0:1, 0:128]).then_inc(P.sems["ev"], 1)
                p.seg("scalar", provides={"ev": 1})(tr2e)

            def thr_rep(eng, P):
                eng.matmul(ps_s2[0:16, 0:CH], ones_row[0:1, 0:16],
                           thr_row[0:1, 0:CH], start=True,
                           stop=True).then_inc(P.sems["mm"], 1)
            p.seg("tensor", provides={"mm": 1})(thr_rep)

            def v_mask(eng, P):
                eng.tensor_tensor(out=mask_cm[:], in0=lgt[:],
                                  in1=ps_s2[0:16, 0:CH],
                                  op=ALU.is_ge).then_inc(P.sems["tick"], 1)
            p.seg("vector", provides={"tick": 1})(v_mask)

            emit_stream_mm(
                p, src=lambda g: w1T_t[:, (g * 4) * KD * 128:
                                       (g + 1) * 4 * KD * 128],
                bufs=wb, Mt=KH, Kt=KD, N=CH,
                rhs=lambda ki: xn1[:, ki * CH:(ki + 1) * CH],
                evac=lambda mi, eng, bank: eng.activation(
                    out=hbb[:, mi * CH:(mi + 1) * CH], in_=bank[:, 0:CH],
                    func=AF.Relu, bias=vb32["b1"][:, mi:mi + 1]),
                ps=ps, group_mt=4)

            mm0, ev0 = p.base("mm"), p.base("ev")
            tk0 = p.tot["tk_vector"]
            w0 = dict(p.tot)

            def mrep_t(eng, P, mm0=mm0, tk0=tk0):
                for e in range(16):
                    if e >= 2:
                        eng.wait_ge(P.sems["tk_vector"], tk0 + e - 1)
                    eng.matmul(ps_rR[:, 0:CH] if e % 2 == 0 else ps_rM[:, 0:CH],
                               oh[:, e * 128:(e + 1) * 128],
                               mask_cm[:], start=True,
                               stop=True).then_inc(P.sems["mm"], 1)
            p.seg("tensor", provides={"mm": 16}, waits=w0)(mrep_t)

            def mmask_v(eng, P, mm0=mm0):
                for e in range(16):
                    eng.wait_ge(P.sems["mm"], mm0 + e + 1)
                    bank = ps_rR if e % 2 == 0 else ps_rM
                    for j in range(2):
                        ti = e * 2 + j
                        ins = eng.tensor_mul(
                            out=hm(ti), in0=hbb[:, ti * CH:(ti + 1) * CH],
                            in1=bank[:, 0:CH])
                    ins.then_inc(P.sems["tick"], 1)
            p.seg("vector", provides={"tick": 16}, waits=w0)(mmask_v)

            emit_stream_mm(
                p, src=lambda g: w2T_t[:, g * KH * 128:(g + 1) * KH * 128],
                bufs=wb, Mt=KD, Kt=KH, N=CH,
                rhs=lambda ki: hm(ki),
                evac=lambda mi, eng, bank: eng.activation(
                    out=o_f[:, mi * CH:(mi + 1) * CH], in_=bank[:, 0:CH],
                    func=AF.Identity, bias=vb["b2"][:, mi:mi + 1]),
                ps=ps, group_mt=1)

            def v_asm(eng, P):
                for ki in range(KD):
                    sl = slice(ki * CH, (ki + 1) * CH)
                    eng.tensor_add(out=tmp_t[:], in0=x1f[:, sl], in1=xn1[:, sl])
                    eng.drain()
                    eng.scalar_tensor_tensor(
                        out=tmp2_t[:], in0=p_bf[:, sl],
                        scalar=vb["g1"][:, ki:ki + 1], in1=tmp_t[:],
                        op0=ALU.mult, op1=ALU.add)
                    eng.drain()
                    eng.scalar_tensor_tensor(
                        out=tmp_t[:], in0=m_bf[:, sl],
                        scalar=vb["g2"][:, ki:ki + 1], in1=tmp2_t[:],
                        op0=ALU.mult, op1=ALU.add)
                    eng.drain()
                    ins = eng.tensor_add(out=x1f[:, sl], in0=tmp_t[:],
                                         in1=o_f[:, sl])
                    eng.drain()
                ins.then_inc(P.sems["tick"], 1)
            p.seg("vector", provides={"tick": 1})(v_asm)

            def st_x2(eng, P, ch=ch):
                for ki in range(KD):
                    _dma(eng, P, "dms",
                         x2_o[ki * 128:(ki + 1) * 128, ch * CH:(ch + 1) * CH],
                         x1f[:, ki * CH:(ki + 1) * CH])
            p.seg("sync", provides={"dms": 16 * KD})(st_x2)

        p.build(ctx)
    return nc


def _pack_v32(v):
    return np.ascontiguousarray(v.reshape(KH, 128).T)


def _l4_inmaps(inputs, X1, wbf):
    pref = "L1_attn_"
    NW = KH * KD * 128
    shared = {
        "projT_t": _w_image(np.ascontiguousarray(inputs[pref + "proj_w"].T)),
        "fc1T_t": np.ascontiguousarray(wbf[:, 0:NW]),
        "fc2T_t": np.ascontiguousarray(wbf[:, NW:2 * NW]),
        "w1T_t": np.ascontiguousarray(wbf[:, 2 * NW:3 * NW]),
        "w2T_t": np.ascontiguousarray(wbf[:, 3 * NW:4 * NW]),
        "wr_t": np.ascontiguousarray(
            inputs["L1_ffn_w_router"].reshape(KD, 128, 16).transpose(1, 0, 2)
            .reshape(128, KD * 16)),
        "oh_t": np.ascontiguousarray(
            (np.arange(16)[:, None] == np.repeat(np.arange(16), 128)[None, :])
            .astype(np.float32)),
        "id_t": np.eye(128, dtype=np.float32),
        "i_proj_b": _pack_v(inputs[pref + "proj_b"]),
        "i_ln1_g": _pack_v(inputs[pref + "ln1_g"]),
        "i_ln1_b": _pack_v(inputs[pref + "ln1_b"]),
        "i_ln2_g": _pack_v(inputs[pref + "ln2_g"]),
        "i_ln2_b": _pack_v(inputs[pref + "ln2_b"]),
        "i_fc2_b": _pack_v(inputs[pref + "fc2_b"]),
        "i_b2": _pack_v(inputs["L1_ffn_b2"]),
        "i_g1": _pack_v(inputs[pref + "gamma1"]),
        "i_g2": _pack_v(inputs[pref + "gamma2"]),
        "i_lnh_g": _pack_v32(inputs[pref + "lnh_g"]),
        "i_lnh_b": _pack_v32(inputs[pref + "lnh_b"]),
        "i_fc1_b": _pack_v32(inputs[pref + "fc1_b"]),
        "i_b1": _pack_v32(inputs["L1_ffn_b1"]),
        "i_brt": np.ascontiguousarray(
            np.pad(inputs["L1_ffn_b_router"], (0, 112)).reshape(1, 128).T),
    }
    maps = []
    for c in range(NCORE):
        b, h = c // 2, c % 2
        m = dict(shared)
        m["x1_p"] = np.ascontiguousarray(X1[b][:, h * TPC:(h + 1) * TPC])
        maps.append(m)
    return maps


_L4_NC = None


def run_l4(inputs, X1, wbf):
    global _L4_NC
    if _L4_NC is None:
        _L4_NC = build_l4()
    res = run_bass_kernel_spmd(_L4_NC, _l4_inmaps(inputs, X1, wbf),
                               core_ids=list(range(NCORE)))
    return res.results


def kernel(**inputs):
    x = np.asarray(inputs["L0_attn_x"], np.float32)
    x_cm = np.ascontiguousarray(x.transpose(0, 2, 1))

    r1 = run_l1(inputs, x_cm)
    WBF = np.concatenate([np.asarray(r1[c]["l4wb"]) for c in range(NCORE)],
                         axis=1)
    DELTA = np.stack([np.concatenate(
        [np.asarray(r1[2 * b]["delta_o"]), np.asarray(r1[2 * b + 1]["delta_o"])],
        axis=1) for b in range(B)])
    XCONV = np.stack([np.concatenate(
        [np.asarray(r1[2 * b]["xconv_o"]), np.asarray(r1[2 * b + 1]["xconv_o"])],
        axis=1) for b in range(B)])
    SZ = np.stack([np.concatenate(
        [np.asarray(r1[2 * b]["siluz_o"]), np.asarray(r1[2 * b + 1]["siluz_o"])],
        axis=1) for b in range(B)])
    BCF = np.stack([np.concatenate(
        [np.asarray(r1[2 * b]["bc_o"]), np.asarray(r1[2 * b + 1]["bc_o"])],
        axis=1) for b in range(B)])

    r2 = run_l2(inputs, DELTA, XCONV, BCF)
    Y = np.stack([np.concatenate(
        [np.asarray(r2[2 * b]["y_o"]), np.asarray(r2[2 * b + 1]["y_o"])],
        axis=0) for b in range(B)])

    r3 = run_l3(inputs, x_cm, Y, SZ)
    X1 = np.stack([np.concatenate(
        [np.asarray(r3[2 * b]["x1_o"]), np.asarray(r3[2 * b + 1]["x1_o"])],
        axis=1) for b in range(B)])

    r4 = run_l4(inputs, X1, WBF)
    X2 = np.stack([np.concatenate(
        [np.asarray(r4[2 * b]["x2_o"]), np.asarray(r4[2 * b + 1]["x2_o"])],
        axis=1) for b in range(B)])
    return np.ascontiguousarray(X2.transpose(0, 2, 1)).astype(np.float32)
